# version 04 (final)
# baseline (speedup 1.0000x reference)
import math
import contextlib
import numpy as np
import ml_dtypes
import concourse.bass as bass
import concourse.mybir as mybir
from concourse.bass_utils import run_bass_kernel_spmd

F32 = mybir.dt.float32
BF16 = mybir.dt.bfloat16
I32 = mybir.dt.int32
AF = mybir.ActivationFunctionType
ALU = mybir.AluOpType

B, L, D, F, E, R, N2 = 8, 4096, 1024, 4096, 8, 16, 32
Q = 128
NCHUNK = L // Q
DFTN = 255
FH = 128
SBT = 512
NSB = L // SBT
NDG = 4
DG = D // NDG
TWOPI = 2.0 * math.pi

_DT_BYTES = {BF16: 2, F32: 4, I32: 4}


class Emit:

    def __init__(self, nc):
        self.nc = nc
        self.q = {e: [] for e in ("sync", "scalar", "vector", "tensor", "gpsimd")}
        self.cnt = {}

    def bump(self, sem, k):
        self.cnt[id(sem)] = self.cnt.get(id(sem), 0) + k
        return self.cnt[id(sem)]

    def val(self, sem):
        return self.cnt.get(id(sem), 0)

    def op(self, eng, builder, wait=(), inc=None, inc_by=1):
        tgt = self.bump(inc, inc_by) if inc is not None else None
        wait = tuple((s, v) for (s, v) in wait)

        def run(e, builder=builder, wait=wait, inc=inc, inc_by=inc_by):
            for (s, v) in wait:
                e.wait_ge(s, v)
            ins = builder(e)
            if inc is not None:
                ins.then_inc(inc, inc_by)

        self.q[eng].append(run)
        return tgt

    def wait(self, eng, sem, v):
        self.q[eng].append(lambda e, s=sem, v=v: e.wait_ge(s, v))

    def ve_fence(self, c):
        return self.op("vector", lambda e: e.tensor_copy(c.k_fence[:], c.k_eps[:]),
                       inc=c.VE)

    def ac_fence(self, c):
        return self.op("scalar", lambda e: e.copy(c.k_fence2[:], c.k_eps[:]),
                       inc=c.AC)

    def drain(self, eng):
        self.q[eng].append(lambda e: e.drain())

    def drain_inc(self, eng, sem, k=1):
        tgt = self.bump(sem, k)
        self.q[eng].append(lambda e, s=sem, k=k: e.drain(fusable=False).then_inc(s, k))
        return tgt

    def run_all(self, block):
        @block.sync
        def _(e):
            for f in self.q["sync"]:
                f(e)

        @block.scalar
        def _(e):
            for f in self.q["scalar"]:
                f(e)

        @block.vector
        def _(e):
            for f in self.q["vector"]:
                f(e)

        @block.tensor
        def _(e):
            for f in self.q["tensor"]:
                f(e)

        @block.gpsimd
        def _(e):
            for f in self.q["gpsimd"]:
                f(e)


class Arena:

    def __init__(self, nc, es, name, nbytes):
        self.nbytes = nbytes
        self.base = es.enter_context(nc.sbuf_tensor(name, [128, nbytes // 2], BF16))
        self.off = 0
        self.marks = []
        self.peak = 0

    def mark(self):
        self.marks.append(self.off)

    def release(self):
        self.off = self.marks.pop()

    def tile(self, rows, cols, dt=BF16):
        nb = _DT_BYTES[dt] * cols
        self.off = (self.off + 63) & ~63
        o = self.off
        self.off += nb
        self.peak = max(self.peak, self.off)
        assert self.off <= self.nbytes, f"arena overflow {self.off} > {self.nbytes}"
        if dt == BF16:
            return self.base[0:rows, o // 2: o // 2 + cols]
        return self.base[0:rows, o // 2: o // 2 + nb // 2].bitcast(dt)



def host_constants():
    r = np.arange(Q, dtype=np.float64)
    f = np.arange(FH, dtype=np.float64)
    ang = TWOPI * np.outer(r, f) / DFTN
    dft_wre = np.cos(ang).astype(np.float32)
    dft_wim = (-np.sin(ang)).astype(np.float32)
    cf = np.full(FH, 2.0 / DFTN)
    cf[0] = 1.0 / DFTN
    j = np.arange(Q, dtype=np.float64)
    ang2 = TWOPI * np.outer(f, j) / DFTN
    idft_vre = (cf[:, None] * np.cos(ang2)).astype(np.float32)
    idft_vim = (-cf[:, None] * np.sin(ang2)).astype(np.float32)
    ident = np.eye(128, dtype=np.float32)
    sel8 = np.zeros((32, 128), dtype=np.float32)
    for e in range(8):
        sel8[e, e * 16:(e + 1) * 16] = 1.0
    ones8 = np.zeros((32, 128), dtype=np.float32)
    ones8[0:8, :] = 1.0
    jrev = (np.float32(Q - 1) - np.arange(Q, dtype=np.float32)).reshape(Q, 1)
    onesrow = np.ones((2, 128), dtype=np.float32)
    tramp = np.arange(1, Q + 1, dtype=np.float32).reshape(1, Q)
    jrevrow = jrev.reshape(1, Q)
    shV = np.zeros((1, 512), np.float32)
    shC = np.zeros((1, 512), np.float32)
    for ch8 in range(8):
        shV[0, ch8 * 64: ch8 * 64 + 32] = 0.25
        shC[0, ch8 * 64: ch8 * 64 + 32] = 0.25
        shC[0, ch8 * 64 + 32: ch8 * 64 + 64] = 0.5
    return dict(c_dft_wre=dft_wre, c_dft_wim=dft_wim, c_idft_vre=idft_vre,
                c_idft_vim=idft_vim, c_ident=ident, c_sel8=sel8,
                c_ones8=ones8, c_jrev=jrev, c_onesrow=onesrow, c_tramp=tramp,
                c_jrevrow=jrevrow, c_shV=shV.astype(ml_dtypes.bfloat16),
                c_shC=shC.astype(ml_dtypes.bfloat16))


def host_prepare(inputs):
    gx = np.ascontiguousarray(np.asarray(inputs["L0_attn_x"], dtype=np.float32))
    W = np.asarray(inputs["L0_attn_W"], np.float32)
    bb = np.asarray(inputs["L0_attn_b"], np.float32)
    gw = np.asarray(inputs["L0_ffn_gate_w"], np.float32)
    wg = np.asarray(inputs["L0_ffn_w_gate"], np.float32)
    wu = np.asarray(inputs["L0_ffn_w_up"], np.float32)
    wd = np.asarray(inputs["L0_ffn_w_down"], np.float32)
    A1 = np.asarray(inputs["L0_ffn_A1"], np.float32)
    B1 = np.asarray(inputs["L0_ffn_B1"], np.float32)
    A3 = np.asarray(inputs["L0_ffn_A3"], np.float32)
    B3 = np.asarray(inputs["L0_ffn_B3"], np.float32)
    A2 = np.asarray(inputs["L0_ffn_A2"], np.float32)
    B2 = np.asarray(inputs["L0_ffn_B2"], np.float32)

    def pcol(v):
        return np.ascontiguousarray(v.reshape(8, 128).T)

    def pmat(m):
        n = m.shape[1]
        return np.ascontiguousarray(
            m.reshape(8, 128, n).transpose(1, 0, 2).reshape(128, 8 * n))

    shared = dict(
        p_logdt=pcol(np.asarray(inputs["L0_attn_log_dt"], np.float32)),
        p_lar=pmat(np.asarray(inputs["L0_attn_log_A_real"], np.float32)),
        p_aim=pmat(np.asarray(inputs["L0_attn_A_imag"], np.float32)),
        p_cre=pmat(np.asarray(inputs["L0_attn_C_real"], np.float32)),
        p_cim=pmat(np.asarray(inputs["L0_attn_C_imag"], np.float32)),
        w_Wt=np.ascontiguousarray(W.T),
        p_b=pcol(bb),
        w_brow=np.ascontiguousarray(bb.reshape(1, D)),
        p_logdtc=pmat(np.ascontiguousarray(np.broadcast_to(
            np.asarray(inputs["L0_attn_log_dt"], np.float32)[:, None], (D, N2)))),
        w_gwT=np.ascontiguousarray(gw.T),
        w_wgT=np.ascontiguousarray(wg.T),
        w_wuT=np.ascontiguousarray(wu.T),
        w_wdT=np.ascontiguousarray(wd.T),
        w_A1p=np.ascontiguousarray(A1.transpose(2, 0, 1).reshape(D, 128)),
        w_A3p=np.ascontiguousarray(A3.transpose(2, 0, 1).reshape(D, 128)),
        w_B1s=np.ascontiguousarray(B1.transpose(0, 2, 1).reshape(128, F)),
        w_B3s=np.ascontiguousarray(B3.transpose(0, 2, 1).reshape(128, F)),
        w_A2sT=np.ascontiguousarray(A2.transpose(2, 0, 1).reshape(F, 128)),
        w_B2s=np.ascontiguousarray(B2.transpose(0, 2, 1).reshape(128, D)),
    )
    shared.update(host_constants())
    return [dict(shared, x_seq=np.ascontiguousarray(gx[c])) for c in range(8)]



class Ctx:
    pass


def make_ctx(mode):
    c = Ctx()
    c.mode = mode
    c.nc = bass.Bass("TRN2", target_bir_lowering=False, debug=False)
    c.em = Emit(c.nc)
    c.es = contextlib.ExitStack()
    nc = c.nc

    def din(name, shape, dt=F32):
        return nc.dram_tensor(name, list(shape), dt, kind="ExternalInput").ap()

    c.din = din
    c.dout = lambda name, shape, dt=F32: nc.dram_tensor(
        name, list(shape), dt, kind="ExternalOutput").ap()
    c.dscr = lambda name, shape, dt=F32: nc.dram_tensor(
        name, list(shape), dt, kind="Internal").ap()
    c.sb = lambda name, shape, dt: c.es.enter_context(
        nc.sbuf_tensor(name, list(shape), dt))
    c.psum = lambda name, shape, dt=F32: c.es.enter_context(
        nc.psum_tensor(name, list(shape), dt))

    c.x_seq = din("x_seq", [L, D])
    for nm, shp in [("p_logdt", [128, 8]), ("p_lar", [128, 256]),
                    ("p_aim", [128, 256]), ("p_cre", [128, 256]),
                    ("p_cim", [128, 256]), ("w_Wt", [D, D]), ("p_b", [128, 8]),
                    ("w_gwT", [D, 8]), ("w_wgT", [D, F]), ("w_wuT", [D, F]),
                    ("w_wdT", [F, D]), ("w_A1p", [D, 128]), ("w_A3p", [D, 128]),
                    ("w_B1s", [128, F]), ("w_B3s", [128, F]),
                    ("w_A2sT", [F, 128]), ("w_B2s", [128, D]),
                    ("c_dft_wre", [Q, FH]), ("c_dft_wim", [Q, FH]),
                    ("c_idft_vre", [FH, Q]), ("c_idft_vim", [FH, Q]),
                    ("c_ident", [128, 128]), ("c_sel8", [32, 128]),
                    ("c_ones8", [32, 128]), ("c_jrev", [128, 1]),
                    ("c_onesrow", [2, 128]), ("c_tramp", [1, Q]),
                    ("c_jrevrow", [1, Q]), ("w_brow", [1, D]),
                    ("p_logdtc", [128, 256])]:
        setattr(c, nm, din(nm, shp))
    c.c_shV = din("c_shV", [1, 512], BF16)
    c.c_shC = din("c_shC", [1, 512], BF16)

    c.out_full = c.dout("out", [L, D])

    c.s_wgT_bf = c.dscr("s_wgT_bf", [D, F], BF16)
    c.s_wuT_bf = c.dscr("s_wuT_bf", [D, F], BF16)
    c.s_wdT_bf = c.dscr("s_wdT_bf", [F, D], BF16)
    mk_gy = c.dout if mode == "ssm" else c.dscr
    c.s_gy = mk_gy("s_gy", [L, D], BF16)
    c.s_xn1 = c.dscr("s_xn1", [L, D], BF16)
    c.s_WT_bf = c.dscr("s_WT_bf", [D, D], BF16)
    for nm in ("s_bh", "s_bm", "s_bl", "s_ah", "s_al",
               "s_gh", "s_gl", "s_mh", "s_ml"):
        setattr(c, nm, c.dscr(nm, [128, 512], BF16))
    c.s_lqre = c.dscr("s_lqre", [128, 256])
    c.s_lqim = c.dscr("s_lqim", [128, 256])
    c.s_lqimn = c.dscr("s_lqimn", [128, 256])
    c.s_lvre = c.dscr("s_lvre", [128, 256], BF16)
    c.s_lvim = c.dscr("s_lvim", [128, 256], BF16)
    if mode == "ffn_ext":
        c.s_xn2 = din("s_xn2", [L, D], BF16)
        c.s_x2 = din("s_x2", [L, D])
    elif mode == "p2":
        c.s_xn2 = c.dout("s_xn2", [L, D], BF16)
        c.s_x2 = c.dout("s_x2", [L, D])
    else:
        c.s_xn2 = c.dscr("s_xn2", [L, D], BF16)
        c.s_x2 = c.dscr("s_x2", [L, D])
    c.s_flat = c.dscr("s_flat", [4, 8192])

    for nm in ("LD", "PE", "VE", "AC", "GP", "ST", "SC", "LDX", "LDX2", "LDW",
               "LDD", "LDR", "LDG", "LDG2", "STP", "STP2", "STN", "STN2"):
        setattr(c, nm, c.es.enter_context(nc.semaphore(nm)))

    c.ar = Arena(nc, c.es, "arena", 166 * 1024)

    sb = c.sb
    c.k_ident = sb("k_ident", [128, 128], BF16)
    c.k_identf = sb("k_identf", [128, 128], F32)
    c.k_sel8 = sb("k_sel8", [32, 128], BF16)
    c.k_ones8 = sb("k_ones8", [32, 128], BF16)
    c.k_dft_wre = sb("k_dft_wre", [Q, FH], BF16)
    c.k_dft_wim = sb("k_dft_wim", [Q, FH], BF16)
    c.k_idft_vre = sb("k_idft_vre", [FH, Q], BF16)
    c.k_idft_vim = sb("k_idft_vim", [FH, Q], BF16)
    c.k_jrev = sb("k_jrev", [128, 1], F32)
    c.k_onesrow = sb("k_onesrow", [2, 128], BF16)
    c.k_tramp = sb("k_tramp", [1, Q], F32)
    c.k_b = sb("k_b", [128, 8], F32)
    c.k_gwT = sb("k_gwT", [128, 8 * 128], BF16)
    c.k_A1p = sb("k_A1p", [128, 1024], BF16)
    c.k_A3p = sb("k_A3p", [128, 1024], BF16)
    c.k_B1s = sb("k_B1s", [128, F], BF16)
    c.k_B3s = sb("k_B3s", [128, F], BF16)
    c.k_A2sT = sb("k_A2sT", [128, F], BF16)
    c.k_B2s = sb("k_B2s", [128, D], BF16)
    c.k_eps = sb("k_eps", [128, 1], F32)
    c.k_fence = sb("k_fence", [128, 1], F32)
    c.k_fence2 = sb("k_fence2", [128, 1], F32)
    c.k_npi = sb("k_npi", [128, 1], F32)
    c.k_affV = sb("k_affV", [4, 128], BF16)
    c.k_affT = sb("k_affT", [6, 128], BF16)
    c.k_affM = sb("k_affM", [4, 128], BF16)
    c.k_brow = sb("k_brow", [1, D], BF16)
    c.k_ones128 = sb("k_ones128", [128, 128], BF16)
    c.k_bias128 = sb("k_bias128", [128, D], BF16)

    c.pb = [c.psum(f"pb{i}", [128, 512]) for i in range(7)]
    c.pt = c.psum("pt", [128, 1024], BF16)
    return c



def prelude(c):
    em, LD, VE, SC = c.em, c.LD, c.VE, c.SC
    c.ar.mark()
    stg = c.ar.tile(128, 4096, F32)
    stgb = c.ar.tile(128, 4096, BF16)

    em.op("vector", lambda e: e.memset(c.k_eps[:], 1e-5), inc=VE)
    _half = c.ar.base.shape[1] // 2
    em.op("vector", lambda e, h=_half: e.memset(c.ar.base[0:128, 0:h], 0.0), inc=VE)
    em.op("vector", lambda e, h=_half: e.memset(c.ar.base[0:128, h:2 * h], 0.0), inc=VE)
    for dst, src in [(c.k_jrev, c.c_jrev), (c.k_tramp, c.c_tramp),
                     (c.k_b, c.p_b), (c.k_identf, c.c_ident)]:
        em.op("sync", lambda e, d=dst, s=src: e.dma_start(out=d[:], in_=s),
              inc=LD, inc_by=16)

    small = [
        (c.k_ident, c.c_ident, 128, 128, 1.0),
        (c.k_sel8, c.c_sel8, 32, 128, 1.0),
        (c.k_ones8, c.c_ones8, 32, 128, 1.0),
        (c.k_onesrow, c.c_onesrow, 2, 128, 1.0),
        (c.k_dft_wre, c.c_dft_wre, Q, FH, 1.0),
        (c.k_dft_wim, c.c_dft_wim, Q, FH, 1.0),
        (c.k_idft_vre, c.c_idft_vre, FH, Q, 1.0),
        (c.k_idft_vim, c.c_idft_vim, FH, Q, 1.0),
        (c.k_B1s, c.w_B1s, 128, F, 2.0),
        (c.k_B3s, c.w_B3s, 128, F, 2.0),
        (c.k_B2s, c.w_B2s, 128, D, 2.0),
    ]
    for dst, src, rows, cols, scale in small:
        ld = em.op("sync", lambda e, s=src, r=rows, co=cols:
                   e.dma_start(out=stg[0:r, 0:co], in_=s),
                   wait=[(VE, em.val(VE))], inc=LD, inc_by=16)
        if scale != 1.0:
            em.op("vector", lambda e, d=dst, r=rows, co=cols, sc=scale:
                  e.tensor_scalar_mul(d[0:r, 0:co], stg[0:r, 0:co], sc),
                  wait=[(LD, ld)], inc=VE)
        else:
            em.op("vector", lambda e, d=dst, r=rows, co=cols:
                  e.tensor_copy(d[0:r, 0:co], stg[0:r, 0:co]),
                  wait=[(LD, ld)], inc=VE)

    em.op("vector", lambda e: e.memset(c.k_gwT[:], 0.0), inc=VE)
    em.op("vector", lambda e: e.memset(c.k_npi[:], -math.pi), inc=VE)
    for r in range(3):
        em.op("gpsimd", lambda e, r=r: e.dma_start(out=c.k_affV[r:r + 1, :],
                                                   in_=c.c_jrevrow), inc=c.LDG, inc_by=16)
        em.op("gpsimd", lambda e, r=r: e.dma_start(out=c.k_affT[r:r + 1, :],
                                                   in_=c.c_tramp), inc=c.LDG, inc_by=16)
    em.op("gpsimd", lambda e: e.dma_start(out=c.k_affV[3:4, :],
                                          in_=c.c_onesrow[0:1, :]), inc=c.LDG, inc_by=16)
    for r in range(3, 6):
        em.op("gpsimd", lambda e, r=r: e.dma_start(out=c.k_affT[r:r + 1, :],
                                                   in_=c.c_onesrow[0:1, :]),
              inc=c.LDG, inc_by=16)
    for r in range(2):
        em.op("gpsimd", lambda e, r=r: e.dma_start(out=c.k_affM[r:r + 1, :],
                                                   in_=c.c_tramp), inc=c.LDG, inc_by=16)
        em.op("gpsimd", lambda e, r=r: e.dma_start(out=c.k_affM[r + 2:r + 3, :],
                                                   in_=c.c_onesrow[0:1, :]),
              inc=c.LDG, inc_by=16)
    em.op("gpsimd", lambda e: e.dma_start(out=c.k_brow[0:1, :], in_=c.w_brow),
          inc=c.LDG, inc_by=16)
    em.op("vector", lambda e: e.memset(c.k_ones128[:], 0.0), inc=VE)
    em.op("vector", lambda e: e.memset(c.k_bias128[:], 0.0), inc=VE)
    em.drain("vector")
    vb = em.ve_fence(c)
    em.wait("gpsimd", VE, vb)
    em.op("gpsimd", lambda e: e.dma_start(out=c.k_ones128[0:1, :],
                                          in_=c.c_onesrow[0:1, :]), inc=c.LDG, inc_by=16)
    em.op("gpsimd", lambda e: e.dma_start(out=c.k_bias128[0:1, :], in_=c.w_brow),
          inc=c.LDG, inc_by=16)
    for dst, src, m in [(c.k_A1p, c.w_A1p, 128), (c.k_A3p, c.w_A3p, 128)]:
        for k in range(8):
            ld = em.op("sync", lambda e, s=src, k=k, m=m:
                       e.dma_start(out=stg[:, 0:m], in_=s[k * 128:(k + 1) * 128, :]),
                       wait=[(VE, em.val(VE))], inc=LD, inc_by=16)
            em.op("vector", lambda e, d=dst, k=k, m=m:
                  e.tensor_copy(d[:, k * m:(k + 1) * m], stg[:, 0:m]),
                  wait=[(LD, ld)], inc=VE)
    for k in range(8):
        ld = em.op("sync", lambda e, k=k:
                   e.dma_start(out=stg[:, 0:8], in_=c.w_gwT[k * 128:(k + 1) * 128, :]),
                   wait=[(VE, em.val(VE))], inc=LD, inc_by=16)
        em.op("vector", lambda e, k=k:
              e.tensor_copy(c.k_gwT[:, k * 128:k * 128 + 8], stg[:, 0:8]),
              wait=[(LD, ld)], inc=VE)
    for k in range(32):
        ld = em.op("sync", lambda e, k=k:
                   e.dma_start(out=stg[:, 0:128], in_=c.w_A2sT[k * 128:(k + 1) * 128, :]),
                   wait=[(VE, em.val(VE))], inc=LD, inc_by=16)
        em.op("vector", lambda e, k=k:
              e.tensor_copy(c.k_A2sT[:, k * 128:(k + 1) * 128], stg[:, 0:128]),
              wait=[(LD, ld)], inc=VE)

    if getattr(c, "skip_bigcast", False):
        c.ar.release()
        c.prelude_sc = em.val(SC)
        c.prelude_ve = em.val(VE)
        return
    for src, dscr, nrows, cols in [(c.w_wgT, c.s_wgT_bf, D, F),
                                   (c.w_wuT, c.s_wuT_bf, D, F),
                                   (c.w_wdT, c.s_wdT_bf, F, D),
                                   (c.w_Wt, c.s_WT_bf, D, D)]:
        for k in range(nrows // 128):
            ld = em.op("sync", lambda e, s=src, k=k, co=cols:
                       e.dma_start(out=stg[:, 0:co], in_=s[k * 128:(k + 1) * 128, :]),
                       wait=[(SC, em.val(SC)), (VE, em.val(VE))], inc=LD, inc_by=16)
            em.op("vector", lambda e, co=cols:
                  e.tensor_copy(stgb[:, 0:co], stg[:, 0:co]),
                  wait=[(LD, ld)], inc=VE)
            cv = em.ve_fence(c)
            em.op("scalar", lambda e, d=dscr, k=k, co=cols:
                  e.dma_start(out=d[k * 128:(k + 1) * 128, :], in_=stgb[:, 0:co]),
                  wait=[(VE, cv)], inc=SC, inc_by=16)
    c.ar.release()
    em.wait("tensor", c.LDG, em.val(c.LDG))
    c.prelude_sc = em.val(SC)
    c.prelude_ve = em.val(VE)




def ffn_phase(c):
    em = c.em
    LD, PE, VE, AC, ST = c.LD, c.PE, c.VE, c.AC, c.ST
    pb, pt = c.pb, c.pt
    ar = c.ar
    ar.mark()

    def dbg(name, src, rows, cols, wait=()):
        if not getattr(c, "debug", False):
            return
        dram = c.dout("d_" + name, [rows, cols])
        c._dbg_i += 1
        di = c._dbg_i % 2
        tile = c._dbg_tiles[di]
        if c._dbg_st[di] is not None:
            wait = tuple(wait) + ((ST, c._dbg_st[di]),)
        em.op("vector", lambda e: e.tensor_copy(tile[0:rows, 0:cols],
                                                src[0:rows, 0:cols]),
                   wait=wait, inc=VE)
        cv = em.ve_fence(c)
        c._dbg_st[di] = em.op("sync",
                              lambda e: e.dma_start(out=dram, in_=tile[0:rows, 0:cols]),
                              wait=[(VE, cv)], inc=ST, inc_by=16)

    mix = ar.tile(128, 32 * SBT, BF16)
    ffnout = ar.tile(128, 8 * SBT, BF16)
    xn2Ts = [ar.tile(128, 8 * SBT, BF16) for _ in range(2)]
    t1f = ar.tile(128, SBT, BF16)
    t3f = ar.tile(128, SBT, BF16)
    t1w = [ar.tile(128, SBT, BF16) for _ in range(2)]
    t3w = [ar.tile(128, SBT, BF16) for _ in range(2)]
    mb = [ar.tile(128, SBT, BF16) for _ in range(2)]
    mwb = [ar.tile(128, SBT, BF16) for _ in range(2)]
    rwb = [ar.tile(128, SBT, BF16) for _ in range(2)]
    w1bf = ar.tile(128, SBT, BF16)
    w3bf = ar.tile(128, SBT, BF16)
    gsl = [ar.tile(128, SBT, F32) for _ in range(2)]
    usl = [ar.tile(128, SBT, F32) for _ in range(2)]
    ssl = [ar.tile(128, SBT, BF16) for _ in range(2)]
    psl = [ar.tile(128, SBT, BF16) for _ in range(2)]
    awl = [ar.tile(128, SBT, BF16) for _ in range(2)]
    ysacc = [ar.tile(128, SBT, F32) for _ in range(2)]
    zw = [ar.tile(128, SBT, BF16) for _ in range(2)]
    lgt = ar.tile(128, 32, F32)
    lgtT = ar.tile(32, SBT, BF16)
    m8t = ar.tile(128, 32, F32)
    rw1 = ar.tile(128, 4, F32)
    rw2 = ar.tile(128, 4, F32)
    qexp = ar.tile(128, 4, F32)
    maskt = [ar.tile(128, 128, BF16) for _ in range(2)]
    wmaskt = [ar.tile(128, 128, BF16) for _ in range(2)]
    mTs = [ar.tile(32, SBT, BF16) for _ in range(2)]
    wmTs = [ar.tile(32, SBT, BF16) for _ in range(2)]
    wgt = [ar.tile(128, 1024, BF16) for _ in range(2)]
    wut = [ar.tile(128, 1024, BF16) for _ in range(2)]
    wdt = [ar.tile(128, D, BF16) for _ in range(2)]
    tmtile = [ar.tile(128, D, BF16) for _ in range(4)]
    x2tile = [ar.tile(128, D, F32) for _ in range(2)]
    otile = [ar.tile(128, D, F32) for _ in range(2)]
    lnstat4 = ar.tile(128, 48, F32)
    lnag4 = ar.tile(128, 8, F32)
    lnr4 = ar.tile(128, 4, F32)
    jk512 = ar.tile(128, 512, BF16)

    for s in range(2):
        em.op("vector", lambda e, s=s: e.memset(maskt[s][:], 0.0), inc=VE)
        em.op("vector", lambda e, s=s: e.memset(wmaskt[s][:], 0.0), inc=VE)

    xtok = [ar.tile(128, 1024, BF16) for _ in range(2)]

    def load_xn2T(sbi, buf):
        tok0 = sbi * SBT
        lds = [None, None]
        sx = (c.LDX, c.LDX2)
        lds[0] = em.op("sync", lambda e, t=tok0, b=xtok[0]: e.dma_start(
            out=b[:], in_=c.s_xn2[t:t + 128, :]),
            wait=[(VE, em.val(VE))], inc=sx[0], inc_by=16)
        if sbi == 0:
            em.op("vector", lambda e: e.tensor_copy(c.k_fence[:], c.k_eps[:]),
                  wait=[(sx[0], lds[0])], inc=VE)
            em.ve_fence(c)
        for tt in range(4):
            if tt + 1 < 4:
                lds[(tt + 1) % 2] = em.op("sync", lambda e, t=tok0 + (tt + 1) * 128,
                                          b=xtok[(tt + 1) % 2]: e.dma_start(
                    out=b[:], in_=c.s_xn2[t:t + 128, :]),
                    wait=[(VE, em.val(VE))], inc=sx[(tt + 1) % 2], inc_by=16)
            em.wait("tensor", sx[tt % 2], lds[tt % 2])
            em.wait("tensor", VE, em.val(VE))
            for k in range(8):
                em.op("tensor", lambda e, k=k, b=xtok[tt % 2]: e.transpose(
                    pt[0:128, k * 128:(k + 1) * 128],
                    b[:, k * 128:(k + 1) * 128], c.k_ident[:]))
            mmx = em.drain_inc("tensor", PE)
            em.op("vector", lambda e, tt=tt, b=buf: e.tensor_copy(
                b[:, tt * 128: 7 * SBT + tt * 128 + 128: 1]
                .rearrange("p (k m) -> p k m", m=128)[:, 0:8:1, :]
                if False else
                b[:].rearrange("p (k m) -> p k m", m=SBT)[:, :, tt * 128:tt * 128 + 128],
                pt[0:128, 0:1024].rearrange("p (k m) -> p k m", m=128)),
                wait=[(PE, mmx)], inc=VE)
        return em.ve_fence(c)

    def wait_xn2T(eng, tok):
        em.wait(eng, VE, tok)

    ld_xn2 = [None, None]
    ld_xn2[0] = load_xn2T(0, xn2Ts[0])

    for sbi in range(NSB):
        tok0 = sbi * SBT
        xt = xn2Ts[sbi % 2]
        if ld_xn2[sbi % 2] is None:
            ld_xn2[sbi % 2] = load_xn2T(sbi, xt)
        em.wait("tensor", VE, ld_xn2[sbi % 2])
        if sbi + 1 < NSB and not getattr(c, "no_prefetch", False):
            ld_xn2[(sbi + 1) % 2] = load_xn2T(sbi + 1, xn2Ts[(sbi + 1) % 2])
        elif not getattr(c, "no_prefetch", False):
            pass
        if getattr(c, "no_prefetch", False):
            ld_xn2[sbi % 2] = None if sbi + 2 <= NSB else ld_xn2[sbi % 2]

        if sbi == 0:
            dbg("xnTa", xt[:, 384:512], 128, 128)
            dbg("xnTb", xt[:, 7 * SBT + 384:7 * SBT + 512], 128, 128)
        em.wait("tensor", VE, em.val(VE))
        for tt in range(4):
            for k in range(8):
                em.op("tensor", lambda e, k=k, tt=tt, xt=xt:
                      e.matmul(pb[tt][0:128, 0:32],
                               xt[:, k * SBT + tt * 128: k * SBT + tt * 128 + 128],
                               c.k_gwT[:, k * 128: k * 128 + 32],
                               start=(k == 0), stop=(k == 7)))
        mmg = em.drain_inc("tensor", PE)
        for tt in range(4):
            em.op("vector", lambda e, tt=tt: e.tensor_copy(
                lgt[:, tt * 8:(tt + 1) * 8],
                pb[tt][0:128, 0:32][:, 0:8]),
                wait=[(PE, mmg)] if tt == 0 else (), inc=VE)
        if sbi == 0:
            dbg("lgt", lgt, 128, 32)
            dbg("pb3", pb[3][0:128, 0:32], 128, 32)
            dbg("pb2", pb[2][0:128, 0:32], 128, 32)
        for tt in range(4):
            em.op("vector", lambda e, tt=tt: e.max(
                out=m8t[:, tt * 8:(tt + 1) * 8], in_=lgt[:, tt * 8:(tt + 1) * 8]), inc=VE)
        for tt in range(4):
            em.op("vector", lambda e, tt=tt: e.tensor_tensor(
                qexp[:, tt:tt + 1], m8t[:, tt * 8 + 1:tt * 8 + 2],
                m8t[:, tt * 8:tt * 8 + 1], ALU.subtract), inc=VE)
        for s in range(2):
            for tt in range(4):
                em.op("vector", lambda e, s=s, tt=tt: e.tensor_scalar(
                    maskt[s][:, tt * 32: tt * 32 + 8], lgt[:, tt * 8:tt * 8 + 8],
                    m8t[:, tt * 8 + s:tt * 8 + s + 1], None, ALU.is_equal), inc=VE)
        vq = em.val(VE)
        em.op("scalar", lambda e: e.activation(qexp[:], qexp[:], AF.Exp),
                   wait=[(VE, vq)], inc=AC)
        aq = em.ac_fence(c)
        vv = em.op("vector", lambda e: e.tensor_scalar_add(rw2[:], qexp[:], 1.0),
                   wait=[(AC, aq)], inc=VE)
        em.wait("vector", VE, vv)
        vv = em.op("vector", lambda e: e.reciprocal(rw1[:], rw2[:]), inc=VE)
        em.wait("vector", VE, vv)
        em.op("vector", lambda e: e.tensor_scalar(rw2[:], rw1[:], -1.0, 1.0,
                                                  ALU.mult, ALU.add), inc=VE)
        for s, rwv in ((0, rw1), (1, rw2)):
            for tt in range(4):
                em.op("vector", lambda e, tt=tt, s=s, rwv=rwv:
                      e.tensor_scalar(wmaskt[s][:, tt * 32: tt * 32 + 8],
                                      maskt[s][:, tt * 32: tt * 32 + 8],
                                      rwv[:, tt:tt + 1], None, ALU.mult), inc=VE)
        if sbi == 0:
            dbg("rw1", rw1, 128, 4)
            dbg("rw2", rw2, 128, 4)
        vq = em.ve_fence(c)
        em.wait("tensor", VE, vq)
        for s in range(2):
            for tt in range(4):
                em.op("tensor", lambda e, s=s, tt=tt: e.transpose(
                    pt[0:32, s * 512 + tt * 128: s * 512 + (tt + 1) * 128],
                    maskt[s][:, tt * 32:(tt + 1) * 32], c.k_ident[:]))
        mmt = em.drain_inc("tensor", PE)
        for s in range(2):
            for tt in range(4):
                em.op("vector", lambda e, s=s, tt=tt: e.tensor_copy(
                    mTs[s][0:32, tt * 128:(tt + 1) * 128],
                    pt[0:32, s * 512 + tt * 128: s * 512 + (tt + 1) * 128]),
                    wait=[(PE, mmt)] if (s == 0 and tt == 0) else (), inc=VE)
        vq = em.ve_fence(c)
        em.wait("tensor", VE, vq)
        for s in range(2):
            for tt in range(4):
                em.op("tensor", lambda e, s=s, tt=tt: e.transpose(
                    pt[0:32, s * 512 + tt * 128: s * 512 + (tt + 1) * 128],
                    wmaskt[s][:, tt * 32:(tt + 1) * 32], c.k_ident[:]))
        mmt = em.drain_inc("tensor", PE)
        for s in range(2):
            for tt in range(4):
                em.op("vector", lambda e, s=s, tt=tt: e.tensor_copy(
                    wmTs[s][0:32, tt * 128:(tt + 1) * 128],
                    pt[0:32, s * 512 + tt * 128: s * 512 + (tt + 1) * 128]),
                    wait=[(PE, mmt)] if (s == 0 and tt == 0) else (), inc=VE)
        em.ve_fence(c)
        for s in range(2):
            for (dst, lhs, rhs) in ((mb[s], c.k_sel8, mTs[s]),
                                    (mwb[s], c.k_sel8, wmTs[s]),
                                    (rwb[s], c.k_ones8, wmTs[s])):
                em.op("tensor", lambda e, l=lhs, r=rhs: e.matmul(
                    pb[6][:, 0:512], l[:], r[0:32, :], start=True, stop=True),
                    wait=[(VE, em.val(VE))])
                mm = em.drain_inc("tensor", PE)
                em.op("vector", lambda e, d=dst: e.tensor_copy(d[:], pb[6][:, 0:512]),
                      wait=[(PE, mm)], inc=VE)
        if sbi == 0:
            dbg("mb0", mb[0], 128, 512)
            dbg("rwb0", rwb[0], 128, 512)
            dbg("mwb0", mwb[0], 128, 512)

        for (dst, pack) in ((t1f, c.k_A1p), (t3f, c.k_A3p)):
            em.wait("tensor", VE, em.val(VE))
            for k in range(8):
                em.op("tensor", lambda e, k=k, p=pack, xt=xt:
                      e.matmul(pb[6][:, 0:512], p[:, k * 128:(k + 1) * 128],
                               xt[:, k * SBT:(k + 1) * SBT],
                               start=(k == 0), stop=(k == 7)))
            mm = em.drain_inc("tensor", PE)
            em.op("vector", lambda e, d=dst: e.tensor_copy(d[:], pb[6][:, 0:512]),
                  wait=[(PE, mm)], inc=VE)
        for s in range(2):
            em.op("vector", lambda e, s=s: e.tensor_mul(t1w[s][:], t1f[:], mb[s][:]), inc=VE)
            em.op("vector", lambda e, s=s: e.tensor_mul(t3w[s][:], t3f[:], mb[s][:]), inc=VE)
        vgate = em.ve_fence(c)
        if sbi == 0:
            dbg("t1f", t1f, 128, 512)

        ysacc_done = em.val(VE)
        for fb in range(32):
            fcol = fb * 128
            cwg, nwg = wgt[fb % 2], wgt[(fb + 1) % 2]
            cwu, nwu = wut[fb % 2], wut[(fb + 1) % 2]
            if fb == 0 and sbi == 0:
                l1 = em.op("sync", lambda e, d=cwg: e.dma_start(
                    out=d[:].rearrange("p (k m) -> p k m", k=8),
                    in_=c.s_wgT_bf[:, 0:128].rearrange("(k p) m -> p k m", p=128)),
                    wait=[(c.SC, c.prelude_sc)], inc=c.LDW, inc_by=16)
                l2 = em.op("sync", lambda e, d=cwu: e.dma_start(
                    out=d[:].rearrange("p (k m) -> p k m", k=8),
                    in_=c.s_wuT_bf[:, 0:128].rearrange("(k p) m -> p k m", p=128)),
                    inc=c.LDW, inc_by=16)
                em.wait("tensor", c.LDW, l2)
            if fb < 31:
                fn = fcol + 128
                em.op("sync", lambda e, f=fn, d=nwg: e.dma_start(
                    out=d[:].rearrange("p (k m) -> p k m", k=8),
                    in_=c.s_wgT_bf[:, f:f + 128].rearrange("(k p) m -> p k m", p=128)),
                    wait=[(PE, em.val(PE))], inc=c.LDW, inc_by=16)
                ldn = em.op("sync", lambda e, f=fn, d=nwu: e.dma_start(
                    out=d[:].rearrange("p (k m) -> p k m", k=8),
                    in_=c.s_wuT_bf[:, f:f + 128].rearrange("(k p) m -> p k m", p=128)),
                    inc=c.LDW, inc_by=16)
                em.wait("tensor", c.LDW, ldn)
            elif sbi + 1 < NSB:
                em.op("sync", lambda e, d=nwg: e.dma_start(
                    out=d[:].rearrange("p (k m) -> p k m", k=8),
                    in_=c.s_wgT_bf[:, 0:128].rearrange("(k p) m -> p k m", p=128)),
                    wait=[(PE, em.val(PE))], inc=c.LDW, inc_by=16)
                ldn = em.op("sync", lambda e, d=nwu: e.dma_start(
                    out=d[:].rearrange("p (k m) -> p k m", k=8),
                    in_=c.s_wuT_bf[:, 0:128].rearrange("(k p) m -> p k m", p=128)),
                    inc=c.LDW, inc_by=16)

            em.wait("tensor", AC, em.val(AC))
            if fb == 0:
                em.wait("tensor", VE, vgate)
            for (bank, wt) in ((0, cwg), (1, cwu)):
                for k in range(8):
                    em.op("tensor", lambda e, k=k, bk=bank, w=wt, xt=xt:
                          e.matmul(pb[bk][:, 0:512], w[:, k * 128:(k + 1) * 128],
                                   xt[:, k * SBT:(k + 1) * SBT],
                                   start=(k == 0), stop=(k == 7)))
            em.wait("tensor", VE, em.val(VE))
            for (bank, tw) in ((2, t1w[0]), (3, t1w[1])):
                em.op("tensor", lambda e, bk=bank, tw=tw, f=fcol:
                      e.matmul(pb[bk][:, 0:512], c.k_B1s[:, f:f + 128], tw[:],
                               start=True, stop=True))
            mm_d1 = em.drain_inc("tensor", PE)
            em.op("scalar", lambda e: e.copy(w1bf[:], pb[0][:, 0:512]),
                  wait=[(PE, mm_d1)], inc=AC)
            a2 = em.op("scalar", lambda e: e.copy(w3bf[:], pb[1][:, 0:512]), inc=AC)
            aF = em.ac_fence(c)
            for s in range(2):
                em.op("vector", lambda e, s=s: e.tensor_tensor(
                    gsl[s][:], pb[2 + s][:, 0:512], w1bf[:], ALU.add),
                    wait=[(PE, mm_d1), (AC, aF)] if s == 0 else (), inc=VE)
            vg1 = em.val(VE)
            em.wait("tensor", VE, vg1)
            for (bank, tw) in ((2, t3w[0]), (3, t3w[1])):
                em.op("tensor", lambda e, bk=bank, tw=tw, f=fcol:
                      e.matmul(pb[bk][:, 0:512], c.k_B3s[:, f:f + 128], tw[:],
                               start=True, stop=True))
            mm_d3 = em.drain_inc("tensor", PE)
            for s in range(2):
                em.op("vector", lambda e, s=s: e.tensor_tensor(
                    usl[s][:], pb[2 + s][:, 0:512], w3bf[:], ALU.add),
                    wait=[(PE, mm_d3), (AC, aF)] if s == 0 else (), inc=VE)
            vg = em.ve_fence(c)
            for s in range(2):
                em.op("scalar", lambda e, s=s: e.activation(ssl[s][:], gsl[s][:], AF.Silu),
                      wait=[(VE, vg)] if s == 0 else (), inc=AC)
            asig = em.ac_fence(c)
            for s in range(2):
                em.op("vector", lambda e, s=s: e.tensor_mul(psl[s][:], ssl[s][:], usl[s][:]),
                      wait=[(AC, asig)] if s == 0 else (), inc=VE)
            for s in range(2):
                em.op("vector", lambda e, s=s: e.tensor_mul(awl[s][:], psl[s][:], rwb[s][:]),
                      inc=VE)
            em.op("vector", lambda e, fb=fb: e.tensor_tensor(
                mix[:, fb * SBT: fb * SBT + SBT], awl[0][:], awl[1][:], ALU.add),
                inc=VE)
            vmix = em.val(VE)
            if sbi == 0 and fb == 0:
                dbg("w1bf", w1bf, 128, 512)
                dbg("cwg0", wgt[0], 128, 1024)
                dbg("d1ps", pb[2], 128, 512, wait=[(PE, mm_d1)])
                dbg("gsl0", gsl[0], 128, 512)
                dbg("usl0", usl[0], 128, 512)
                dbg("ssl0", ssl[0], 128, 512)
                dbg("psl0", psl[0], 128, 512)
                dbg("awl0", awl[0], 128, 512)
                dbg("mix0", mix, 128, 512)
                _vsnap = em.val(VE)
                for _eng in ("scalar", "tensor", "sync", "gpsimd"):
                    em.wait(_eng, VE, _vsnap)
            for s in range(2):
                em.op("tensor", lambda e, s=s, fb=fb: e.matmul(
                    pb[4 + s][:, 0:512], c.k_A2sT[:, fb * 128:(fb + 1) * 128],
                    psl[s][:], start=(fb % 8 == 0), stop=(fb % 8 == 7)),
                    wait=(([(VE, vmix)] if s == 0 else []) +
                          ([(VE, ysacc_done)] if (fb % 8 == 0 and s == 0) else [])))
            if fb % 8 == 7:
                mmy = em.drain_inc("tensor", PE)
                for s in range(2):
                    if fb == 7:
                        em.op("vector", lambda e, s=s: e.tensor_copy(
                            ysacc[s][:], pb[4 + s][:, 0:512]),
                            wait=[(PE, mmy)] if s == 0 else (), inc=VE)
                    else:
                        em.op("vector", lambda e, s=s: e.tensor_tensor(
                            ysacc[s][:], pb[4 + s][:, 0:512], ysacc[s][:], ALU.add),
                            wait=[(PE, mmy)] if s == 0 else (), inc=VE)
                ysacc_done = em.val(VE)

        if sbi == 0:
            dbg("ysacc0", ysacc[0], 128, 512)
        for s in range(2):
            em.op("vector", lambda e, s=s: e.tensor_mul(zw[s][:], ysacc[s][:], mwb[s][:]),
                  inc=VE)
        vz = em.ve_fence(c)
        if sbi == 0:
            dbg("zw0", zw[0], 128, 512)

        for half in range(2):
            ldw_t = [None] * 32
            ldw_t[0] = em.op("sync", lambda e: e.dma_start(
                out=wdt[0][:], in_=c.s_wdT_bf[0:128, :]),
                wait=[(PE, em.val(PE))], inc=c.LDD, inc_by=16)
            ldw_t[1] = em.op("sync", lambda e: e.dma_start(
                out=wdt[1][:], in_=c.s_wdT_bf[128:256, :]), inc=c.LDD, inc_by=16)
            if half == 0:
                em.wait("tensor", AC, em.val(AC))
                em.wait("tensor", VE, vz)
            else:
                em.wait("tensor", AC, em.val(AC))
            for k in range(32):
                buf = wdt[k % 2]
                em.wait("tensor", c.LDD, ldw_t[k])
                for db4 in range(4):
                    db = half * 4 + db4
                    em.op("tensor", lambda e, k=k, db=db, db4=db4, b=buf: e.matmul(
                        pb[db4][:, 0:512], b[:, db * 128:(db + 1) * 128],
                        mix[:, k * SBT:(k + 1) * SBT],
                        start=(k == 0), stop=False),
                        inc=PE if db4 == 3 else None)
                if k + 2 < 32:
                    ldw_t[k + 2] = em.op("sync", lambda e, k=k, b=buf: e.dma_start(
                        out=b[:], in_=c.s_wdT_bf[(k + 2) * 128:(k + 3) * 128, :]),
                        wait=[(PE, em.val(PE))], inc=c.LDD, inc_by=16)
            for db4 in range(4):
                db = half * 4 + db4
                for s in range(2):
                    em.op("tensor", lambda e, s=s, db=db, db4=db4: e.matmul(
                        pb[db4][:, 0:512], c.k_B2s[:, db * 128:(db + 1) * 128],
                        zw[s][:], start=False, stop=(s == 1)))
            mmd = em.drain_inc("tensor", PE)
            for db4 in range(4):
                db = half * 4 + db4
                em.op("scalar", lambda e, db=db, db4=db4: e.copy(
                    ffnout[:, db * SBT:(db + 1) * SBT], pb[db4][:, 0:512]),
                    wait=[(PE, mmd)] if db4 == 0 else (), inc=AC)
        aff = em.ac_fence(c)
        if sbi == 0:
            dbg("ffnout0", ffnout, 128, 512, wait=[(AC, aff)])

        ldx = [None, None]
        ldx[0] = em.op("sync", lambda e, t=tok0: e.dma_start(
            out=x2tile[0][:], in_=c.s_x2[t:t + 128, :]),
            wait=[(VE, em.val(VE))], inc=c.LDR, inc_by=16)
        ldx[1] = em.op("sync", lambda e, t=tok0: e.dma_start(
            out=x2tile[1][:], in_=c.s_x2[t + 128:t + 256, :]),
            wait=[(VE, em.val(VE))], inc=c.LDR, inc_by=16)
        for tt in range(4):
            em.wait("tensor", VE, em.val(VE))
            for db in range(8):
                em.op("tensor", lambda e, tt=tt, db=db: e.transpose(
                    pt[0:128, db * 128:(db + 1) * 128],
                    ffnout[:, db * SBT + tt * 128: db * SBT + tt * 128 + 128],
                    c.k_ident[:]),
                    wait=[(AC, aff)] if (tt == 0 and db == 0) else ())
            mmt2 = em.drain_inc("tensor", PE)
            em.op("vector", lambda e, tt=tt: e.tensor_copy(tmtile[tt][:], pt[0:128, 0:1024]),
                  wait=[(PE, mmt2)], inc=VE)
            em.op("vector", lambda e, tt=tt: e.bn_stats(
                lnstat4[:, tt * 12:tt * 12 + 6], tmtile[tt][:, 0:512]), inc=VE)
            em.op("vector", lambda e, tt=tt: e.bn_stats(
                lnstat4[:, tt * 12 + 6:tt * 12 + 12], tmtile[tt][:, 512:1024]), inc=VE)
        for tt in range(4):
            em.op("vector", lambda e, tt=tt: e.bn_aggr(
                lnag4[:, tt * 2:tt * 2 + 2], lnstat4[:, tt * 12:tt * 12 + 12]), inc=VE)
        bn = em.ve_fence(c)
        for tt in range(4):
            em.op("scalar", lambda e, tt=tt: e.activation(
                lnr4[:, tt:tt + 1], lnag4[:, tt * 2 + 1:tt * 2 + 2], AF.Ln,
                bias=c.k_eps[:]),
                wait=[(VE, bn)] if tt == 0 else (), inc=AC)
        em.op("scalar", lambda e: e.copy(jk512[:], tmtile[0][:, 0:512]), inc=AC)
        for tt in range(4):
            em.op("scalar", lambda e, tt=tt: e.activation(
                lnr4[:, tt:tt + 1], lnr4[:, tt:tt + 1], AF.Exp, scale=-0.5), inc=AC)
        a2f = em.ac_fence(c)
        st_tok = [None, None]
        for tt in range(4):
            ttok = tok0 + tt * 128
            ws = [(AC, a2f)] if tt == 0 else []
            if st_tok[tt % 2] is not None:
                ws.append(((c.STP, c.STP2)[tt % 2], st_tok[tt % 2]))
            em.op("vector", lambda e, tt=tt: e.tensor_scalar(
                otile[tt % 2][:], tmtile[tt][:], lnag4[:, tt * 2:tt * 2 + 1],
                lnr4[:, tt:tt + 1], ALU.subtract, ALU.mult),
                wait=ws, inc=VE)
            em.op("vector", lambda e, tt=tt: e.tensor_tensor(
                otile[tt % 2][:], otile[tt % 2][:], x2tile[tt % 2][:], ALU.add),
                wait=[(c.LDR, ldx[tt % 2])], inc=VE)
            fv = em.ve_fence(c)
            st_tok[tt % 2] = em.op("sync", lambda e, t=ttok, tt=tt: e.dma_start(
                out=c.out_full[t:t + 128, :], in_=otile[tt % 2][:]),
                wait=[(VE, fv)], inc=(c.STP, c.STP2)[tt % 2], inc_by=16)
            if tt + 2 < 4:
                ldx[tt % 2] = em.op("sync", lambda e, t=tok0 + (tt + 2) * 128,
                                    b=x2tile[tt % 2]: e.dma_start(
                    out=b[:], in_=c.s_x2[t:t + 128, :]),
                    wait=[(VE, em.val(VE))], inc=c.LDR, inc_by=16)
        if getattr(c, "trunc", 0) == 9:
            break

    ar.release()


def phase_dbg(c, ar):
    em = c.em
    VE, ST = c.VE, c.ST

    def dbg(name, src, rows, cols, wait=()):
        if not getattr(c, "debug", False):
            return
        dram = c.dout("d_" + name, [rows, cols])
        c._dbg_i += 1
        di = c._dbg_i % 2
        tile = c._dbg_tiles[di]
        if c._dbg_st[di] is not None:
            wait = tuple(wait) + ((ST, c._dbg_st[di]),)
        em.op("vector", lambda e: e.tensor_copy(tile[0:rows, 0:cols],
                                                src[0:rows, 0:cols]),
              wait=wait, inc=VE)
        cv = em.ve_fence(c)
        c._dbg_st[di] = em.op("sync",
                              lambda e: e.dma_start(out=dram, in_=tile[0:rows, 0:cols]),
                              wait=[(VE, cv)], inc=ST, inc_by=16)
    return dbg



HCOL = 34
NPAIR = 512


def ssm_phase(c):
    em = c.em
    LD, PE, VE, AC, GP, ST, SC = c.LD, c.PE, c.VE, c.AC, c.GP, c.ST, c.SC
    pb, pt = c.pb, c.pt
    ar = c.ar
    ar.mark()
    dbg = phase_dbg(c, ar)

    H = ar.tile(128, NPAIR * HCOL, BF16)
    ws0 = [(VE, em.val(VE)), (SC, em.val(SC))]
    for o in range(0, NPAIR * HCOL, 512):
        w = min(512, NPAIR * HCOL - o)
        em.op("vector", lambda e, o=o, w=w: e.memset(H[:, o:o + w], 0.0),
              wait=ws0, inc=VE)
        ws0 = []

    ar.mark()
    xt4 = [ar.tile(128, D, F32) for _ in range(2)]
    xnt = [ar.tile(128, D, BF16) for _ in range(2)]
    lns = ar.tile(128, 12, F32)
    lna = ar.tile(128, 2, F32)
    lnr1 = ar.tile(128, 1, F32)
    st_tok = [None, None]
    ld_tok = [None, None]
    sx = (c.LDX, c.LDX2)
    ld_tok[0] = em.op("sync", lambda e: e.dma_start(
        out=xt4[0][:], in_=c.x_seq[0:128, :]),
        wait=[(VE, em.val(VE))], inc=sx[0], inc_by=16)
    for ct in range(32):
        bi = ct % 2
        if ct + 1 < 32:
            ld_tok[(ct + 1) % 2] = em.op("sync", lambda e, ct=ct, b=xt4[(ct + 1) % 2]:
                                         e.dma_start(out=b[:], in_=c.x_seq[
                                             (ct + 1) * 128:(ct + 2) * 128, :]),
                                         wait=[(VE, em.val(VE))],
                                         inc=sx[(ct + 1) % 2], inc_by=16)
        em.op("vector", lambda e, b=xt4[bi]: e.bn_stats(lns[:, 0:6], b[:, 0:512]),
              wait=[(sx[bi], ld_tok[bi])], inc=VE)
        em.op("vector", lambda e, b=xt4[bi]: e.bn_stats(lns[:, 6:12], b[:, 512:1024]),
              inc=VE)
        em.drain("vector")
        em.op("vector", lambda e: e.bn_aggr(lna[:], lns[:]), inc=VE)
        bn = em.drain_inc("vector", VE)
        em.op("scalar", lambda e: e.activation(lnr1[:], lna[:, 1:2], AF.Ln,
                                               bias=c.k_eps[:]),
              wait=[(VE, bn)], inc=AC)
        em.drain("scalar")
        em.op("scalar", lambda e: e.activation(lnr1[:], lnr1[:], AF.Exp,
                                               scale=-0.5), inc=AC)
        af = em.drain_inc("scalar", AC)
        ws = [(AC, af)]
        if st_tok[bi] is not None:
            ws.append(((c.STP, c.STP2)[bi], st_tok[bi]))
        em.op("vector", lambda e, b=xt4[bi], o=xnt[bi]: e.tensor_scalar(
            o[:], b[:], lna[:, 0:1], lnr1[:], ALU.subtract, ALU.mult),
            wait=ws, inc=VE)
        fv = em.ve_fence(c)
        st_tok[bi] = em.op("sync", lambda e, ct=ct, o=xnt[bi]: e.dma_start(
            out=c.s_xn1[ct * 128:(ct + 1) * 128, :], in_=o[:]),
            wait=[(VE, fv)], inc=(c.STP, c.STP2)[bi], inc_by=16)
    em.wait("sync", c.STP, em.val(c.STP))
    em.wait("sync", c.STP2, em.val(c.STP2))
    ar.release()

    lq_re = ar.tile(128, 256, F32)
    lq_im = ar.tile(128, 256, F32)
    lq_imn = ar.tile(128, 256, F32)
    lv_re = ar.tile(128, 256, BF16)
    lv_im = ar.tile(128, 256, BF16)

    ar.mark()
    T = {nm: ar.tile(128, 256, F32) for nm in
         ("dt", "nare", "alphn", "beta", "sinb", "cosb", "mag", "lre", "lim",
          "cpre", "cpim", "argT", "lnm", "ta", "tb", "tc", "td")}
    Tb = {nm: ar.tile(128, 256, BF16) for nm in
          ("bh", "bm", "bl", "ah", "al", "gh", "gl", "mh", "ml")}
    Tb["lvre"] = lv_re
    Tb["lvim"] = lv_im

    def cop(fn, wait=(), eng="vector"):
        return em.op(eng, fn, wait=wait, inc=VE if eng == "vector" else AC)

    ldp = {}
    for nm, src in (("logdtc", c.p_logdtc), ("lar", c.p_lar), ("aim", c.p_aim),
                    ("cre", c.p_cre), ("cim", c.p_cim)):
        t = ar.tile(128, 256, F32)
        ldp[nm] = (t, em.op("sync", lambda e, t=t, s=src: e.dma_start(out=t[:], in_=s),
                            wait=[(VE, em.val(VE))], inc=LD, inc_by=16))
    t_logdt, ld1 = ldp["logdtc"]
    t_lar, ld2 = ldp["lar"]
    t_aim, ld3 = ldp["aim"]
    t_cre, ld4 = ldp["cre"]
    t_cim, ld5 = ldp["cim"]
    cop(lambda e: e.activation(T["dt"][:], t_logdt[:], AF.Exp),
        wait=[(LD, ld1)], eng="scalar")
    cop(lambda e: e.activation(T["nare"][:], t_lar[:], AF.Exp),
        wait=[(LD, ld2)], eng="scalar")
    a1 = em.drain_inc("scalar", AC)
    cop(lambda e: e.tensor_mul(T["alphn"][:], T["dt"][:], T["nare"][:]),
        wait=[(AC, a1), (LD, ld3)])
    cop(lambda e: e.tensor_mul(T["beta"][:], T["dt"][:], t_aim[:]))
    em.drain("vector")
    cop(lambda e: e.tensor_scalar_mul(T["beta"][:], T["beta"][:], 1.0 / TWOPI))
    em.drain("vector")

    fr_c = ar.tile(128, 256, F32)
    MAGIC = 12582912.0

    def angle_reduce(dst, src, eng_drain=True):
        cop(lambda e: e.tensor_scalar_add(fr_c[:], src[:], MAGIC))
        em.drain("vector")
        cop(lambda e: e.tensor_scalar_add(fr_c[:], fr_c[:], -MAGIC))
        em.drain("vector")
        cop(lambda e: e.tensor_tensor(dst[:], src[:], fr_c[:], ALU.subtract))
        em.drain("vector")

    angle_reduce(T["ta"], T["beta"])
    cop(lambda e: e.tensor_scalar_add(T["tb"][:], T["beta"][:], 0.25))
    em.drain("vector")
    angle_reduce(T["tb"], T["tb"])
    vq = em.drain_inc("vector", VE)
    cop(lambda e: e.activation(T["sinb"][:], T["ta"][:], AF.Sin, scale=TWOPI),
        wait=[(VE, vq)], eng="scalar")
    cop(lambda e: e.activation(T["cosb"][:], T["tb"][:], AF.Sin, scale=TWOPI),
        eng="scalar")
    cop(lambda e: e.activation(T["mag"][:], T["alphn"][:], AF.Exp, scale=-1.0),
        eng="scalar")
    a2 = em.drain_inc("scalar", AC)
    cop(lambda e: e.tensor_mul(T["lre"][:], T["mag"][:], T["cosb"][:]),
        wait=[(AC, a2), (LD, ld4)])
    cop(lambda e: e.tensor_mul(T["lim"][:], T["mag"][:], T["sinb"][:]),
        wait=[(LD, ld5)])
    em.drain("vector")
    cop(lambda e: e.tensor_scalar_add(T["ta"][:], T["lre"][:], -1.0))
    em.drain("vector")
    cop(lambda e: e.tensor_mul(T["tb"][:], t_cre[:], T["ta"][:]))
    cop(lambda e: e.tensor_mul(T["tc"][:], t_cim[:], T["lim"][:]))
    em.drain("vector")
    cop(lambda e: e.tensor_tensor(T["tb"][:], T["tb"][:], T["tc"][:], ALU.subtract))
    cop(lambda e: e.tensor_mul(T["tc"][:], t_cre[:], T["lim"][:]))
    cop(lambda e: e.tensor_mul(T["td"][:], t_cim[:], T["ta"][:]))
    em.drain("vector")
    cop(lambda e: e.tensor_tensor(T["tc"][:], T["tc"][:], T["td"][:], ALU.add))
    cop(lambda e: e.tensor_mul(T["ta"][:], T["nare"][:], T["nare"][:]))
    cop(lambda e: e.tensor_mul(T["td"][:], t_aim[:], t_aim[:]))
    em.drain("vector")
    cop(lambda e: e.tensor_tensor(T["ta"][:], T["ta"][:], T["td"][:], ALU.add))
    em.drain("vector")
    vv = cop(lambda e: e.reciprocal(T["td"][:], T["ta"][:]))
    em.wait("vector", VE, vv)
    em.drain("vector")
    cop(lambda e: e.tensor_mul(T["ta"][:], T["nare"][:], T["td"][:]))
    cop(lambda e: e.tensor_mul(T["td"][:], t_aim[:], T["td"][:]))
    em.drain("vector")
    cop(lambda e: e.tensor_scalar_mul(T["ta"][:], T["ta"][:], -1.0))
    cop(lambda e: e.tensor_scalar_mul(T["td"][:], T["td"][:], -1.0))
    em.drain("vector")
    cop(lambda e: e.tensor_mul(T["cpre"][:], T["tb"][:], T["ta"][:]))
    cop(lambda e: e.tensor_mul(T["cpim"][:], T["tc"][:], T["ta"][:]))
    cop(lambda e: e.tensor_mul(T["ta"][:], T["tc"][:], T["td"][:]))
    em.drain("vector")
    cop(lambda e: e.tensor_mul(T["td"][:], T["tb"][:], T["td"][:]))
    em.drain("vector")
    cop(lambda e: e.tensor_tensor(T["cpre"][:], T["cpre"][:], T["ta"][:], ALU.subtract))
    cop(lambda e: e.tensor_tensor(T["cpim"][:], T["cpim"][:], T["td"][:], ALU.add))
    em.drain("vector")
    cop(lambda e: e.tensor_scalar_mul(T["cpre"][:], T["cpre"][:], 2.0))
    cop(lambda e: e.tensor_scalar_mul(T["cpim"][:], T["cpim"][:], 2.0))
    em.drain("vector")
    cop(lambda e: e.tensor_scalar_mul(T["ta"][:], T["cpre"][:], -1.0))
    cop(lambda e: e.tensor_scalar_mul(T["tb"][:], T["cpim"][:], -1.0))
    em.drain("vector")
    cop(lambda e: e.tensor_tensor(T["ta"][:], T["ta"][:], T["cpre"][:], ALU.max))
    cop(lambda e: e.tensor_tensor(T["tb"][:], T["tb"][:], T["cpim"][:], ALU.max))
    em.drain("vector")
    cop(lambda e: e.tensor_tensor(T["tc"][:], T["ta"][:], T["tb"][:], ALU.min))
    cop(lambda e: e.tensor_tensor(T["td"][:], T["ta"][:], T["tb"][:], ALU.max))
    em.drain("vector")
    cop(lambda e: e.tensor_scalar_add(T["td"][:], T["td"][:], 1e-30))
    em.drain("vector")
    vv = cop(lambda e: e.reciprocal(T["td"][:], T["td"][:]))
    em.wait("vector", VE, vv)
    em.drain("vector")
    cop(lambda e: e.tensor_mul(T["tc"][:], T["tc"][:], T["td"][:]))
    vq = em.drain_inc("vector", VE)
    cop(lambda e: e.activation(T["tc"][:], T["tc"][:], AF.Arctan),
        wait=[(VE, vq)], eng="scalar")
    a3 = em.drain_inc("scalar", AC)
    cop(lambda e: e.tensor_scalar_mul(T["tc"][:], T["tc"][:], 1.0 / TWOPI),
        wait=[(AC, a3)])
    cop(lambda e: e.tensor_tensor(T["td"][:], T["tb"][:], T["ta"][:],
                                  ALU.is_gt))
    em.drain("vector")
    cop(lambda e: e.tensor_scalar(T["ta"][:], T["tc"][:], -2.0, 0.25,
                                  ALU.mult, ALU.add))
    em.drain("vector")
    cop(lambda e: e.tensor_mul(T["ta"][:], T["td"][:], T["ta"][:]))
    em.drain("vector")
    cop(lambda e: e.tensor_tensor(T["tc"][:], T["tc"][:], T["ta"][:], ALU.add))
    cop(lambda e: e.tensor_scalar(T["ta"][:], T["cpre"][:], 0.0, None, ALU.is_lt))
    cop(lambda e: e.tensor_scalar(T["tb"][:], T["cpim"][:], 0.0, None, ALU.is_ge))
    em.drain("vector")
    cop(lambda e: e.tensor_scalar(T["tb"][:], T["tb"][:], 2.0, -1.0,
                                  ALU.mult, ALU.add))
    cop(lambda e: e.tensor_scalar(T["td"][:], T["tc"][:], -2.0, 0.5,
                                  ALU.mult, ALU.add))
    em.drain("vector")
    cop(lambda e: e.tensor_mul(T["td"][:], T["ta"][:], T["td"][:]))
    em.drain("vector")
    cop(lambda e: e.tensor_tensor(T["tc"][:], T["tc"][:], T["td"][:], ALU.add))
    em.drain("vector")
    cop(lambda e: e.tensor_mul(T["argT"][:], T["tb"][:], T["tc"][:]))
    cop(lambda e: e.tensor_mul(T["ta"][:], T["cpre"][:], T["cpre"][:]))
    cop(lambda e: e.tensor_mul(T["tb"][:], T["cpim"][:], T["cpim"][:]))
    em.drain("vector")
    cop(lambda e: e.tensor_tensor(T["ta"][:], T["ta"][:], T["tb"][:], ALU.add))
    em.drain("vector")
    cop(lambda e: e.tensor_scalar_add(T["ta"][:], T["ta"][:], 1e-38))
    vq = em.drain_inc("vector", VE)
    cop(lambda e: e.activation(T["lnm"][:], T["ta"][:], AF.Ln), wait=[(VE, vq)],
        eng="scalar")
    a4 = em.drain_inc("scalar", AC)
    cop(lambda e: e.tensor_scalar_mul(T["lnm"][:], T["lnm"][:], 0.5),
        wait=[(AC, a4)])
    cop(lambda e: e.tensor_scalar_mul(T["ta"][:], T["beta"][:], float(Q)))
    em.drain("vector")
    angle_reduce(T["ta"], T["ta"])
    cop(lambda e: e.tensor_scalar(T["tb"][:], T["beta"][:], float(Q), 0.25,
                                  ALU.mult, ALU.add))
    em.drain("vector")
    angle_reduce(T["tb"], T["tb"])
    vq = em.drain_inc("vector", VE)
    cop(lambda e: e.activation(T["ta"][:], T["ta"][:], AF.Sin, scale=TWOPI),
        wait=[(VE, vq)], eng="scalar")
    cop(lambda e: e.activation(T["tb"][:], T["tb"][:], AF.Sin, scale=TWOPI),
        eng="scalar")
    cop(lambda e: e.activation(T["tc"][:], T["alphn"][:], AF.Exp,
                               scale=-float(Q)), eng="scalar")
    cop(lambda e: e.activation(T["td"][:], T["alphn"][:], AF.Exp, scale=1.0),
        eng="scalar")
    a5 = em.drain_inc("scalar", AC)
    cop(lambda e: e.tensor_mul(lq_re[:], T["tc"][:], T["tb"][:]), wait=[(AC, a5)])
    cop(lambda e: e.tensor_mul(lq_im[:], T["tc"][:], T["ta"][:]))
    cop(lambda e: e.tensor_mul(Tb["lvre"][:], T["td"][:], T["cosb"][:]))
    cop(lambda e: e.tensor_mul(T["td"][:], T["td"][:], T["sinb"][:]))
    em.drain("vector")
    cop(lambda e: e.tensor_scalar_mul(lq_imn[:], lq_im[:], -1.0))
    cop(lambda e: e.tensor_scalar_mul(Tb["lvim"][:], T["td"][:], -1.0))
    cop(lambda e: e.tensor_scalar_mul(T["ta"][:], T["alphn"][:], -1.0))
    em.drain("vector")

    def split2(hi, lo, src):
        cop(lambda e: e.tensor_copy(Tb[hi][:], src[:]))
        em.drain("vector")
        cop(lambda e: e.tensor_tensor(T["td"][:], src[:], Tb[hi][:], ALU.subtract))
        em.drain("vector")
        cop(lambda e: e.tensor_copy(Tb[lo][:], T["td"][:]))
        em.drain("vector")

    def split3(hi, mid, lo, src):
        cop(lambda e: e.tensor_copy(Tb[hi][:], src[:]))
        em.drain("vector")
        cop(lambda e: e.tensor_tensor(T["tb"][:], src[:], Tb[hi][:], ALU.subtract))
        em.drain("vector")
        cop(lambda e: e.tensor_copy(Tb[mid][:], T["tb"][:]))
        em.drain("vector")
        cop(lambda e: e.tensor_tensor(T["tb"][:], T["tb"][:], Tb[mid][:], ALU.subtract))
        em.drain("vector")
        cop(lambda e: e.tensor_copy(Tb[lo][:], T["tb"][:]))
        em.drain("vector")

    split3("bh", "bm", "bl", T["beta"])
    split2("ah", "al", T["ta"])
    split2("gh", "gl", T["argT"])
    split2("mh", "ml", T["lnm"])
    vq = em.ve_fence(c)
    em.wait("scalar", VE, vq)
    for nm in ("bh", "bm", "bl", "ah", "al", "gh", "gl", "mh", "ml"):
        dscr = getattr(c, "s_" + nm)
        for part in range(2):
            em.op("scalar", lambda e, t=Tb[nm], d=dscr, p=part: e.dma_start(
                out=d[:, p * 256:(p + 1) * 256], in_=t[:]), inc=SC, inc_by=16)
    sc_params = em.val(SC)
    if getattr(c, "debug", False):
        dbg("cpre", T["cpre"], 128, 256)
        dbg("cpim", T["cpim"], 128, 256)
        dbg("argT", T["argT"], 128, 256)
        dbg("lnm", T["lnm"], 128, 256)
        dbg("lqre", lq_re, 128, 256)
        dbg("beta", T["beta"], 128, 256)
    em.wait("sync", SC, sc_params)
    em.wait("sync", ST, em.val(ST))
    ar.release()

    rhsck = ar.tile(6, 512, BF16)
    stg5 = ar.tile(128, 512, F32)
    sins = ar.tile(128, 512, BF16)


    def affine_chunk(lhsT, krows, rhs_rows, ck, bank, extra_wait=()):
        lds = []
        for ri, (src, kind, G) in enumerate(rhs_rows):
            if kind == "p":
                lds.append(em.op("sync", lambda e, s=src, ri=ri, G=G, ck=ck:
                                 e.dma_start(
                    out=rhsck[ri:ri + 1, :],
                    in_=s.rearrange("p (part b n) -> b p part n",
                                    part=2, n=32)[G][ck * 8:(ck + 1) * 8]),
                    wait=[(PE, em.val(PE))], inc=c.LDW, inc_by=16))
            else:
                lds.append(em.op("sync", lambda e, s=src, ri=ri: e.dma_start(
                    out=rhsck[ri:ri + 1, :], in_=s),
                    wait=[(PE, em.val(PE))], inc=c.LDW, inc_by=16))
        em.wait("tensor", c.LDW, lds[-1])
        for w in extra_wait:
            em.wait("tensor", *w)
        em.op("tensor", lambda e, l=lhsT, k=krows, bank=bank: e.matmul(
            pb[bank][:, 0:512], l[0:k, :], rhsck[0:k, :], start=True, stop=True))
        return em.drain_inc("tensor", PE)

    def build_table(dst, G, ramp_lhs, mag_rows, ang_rows, mag_k, ang_k, nck):
        for ck in range(nck):
            mm = affine_chunk(ramp_lhs[1], mag_k,
                              [(s, "p", G) for s in mag_rows], ck, 5 + ck % 2,
                              extra_wait=[(AC, em.val(AC))])
            em.op("scalar", lambda e, ck=ck, bank=5 + ck % 2: e.activation(
                dst[:, ck * 512:(ck + 1) * 512], pb[bank][:, 0:512], AF.Exp),
                wait=[(PE, mm)], inc=AC)
        acm = em.drain_inc("scalar", AC)
        for ck in range(nck):
            rows = [(s, "p", G) for s in ang_rows[:-1]] + [(ang_rows[-1], "c", G)]
            mm = affine_chunk(ramp_lhs[0], ang_k, rows, ck, 5 + ck % 2,
                              extra_wait=[(VE, em.val(VE))])
            em.op("vector", lambda e, bank=5 + ck % 2: e.tensor_scalar_add(
                stg5[:], pb[bank][:, 0:512], 12582912.0),
                wait=[(PE, mm)], inc=VE)
            em.drain("vector")
            em.op("vector", lambda e: e.tensor_scalar_add(
                stg5[:], stg5[:], -12582912.0), inc=VE)
            em.drain("vector")
            em.op("vector", lambda e, bank=5 + ck % 2: e.tensor_tensor(
                stg5[:], pb[bank][:, 0:512], stg5[:], ALU.subtract), inc=VE)
            sv = em.drain_inc("vector", VE)
            em.op("scalar", lambda e, ck=ck: e.activation(
                sins[:], stg5[:], AF.Sin, scale=TWOPI),
                wait=[(VE, sv), (AC, acm)], inc=AC)
            sa = em.drain_inc("scalar", AC)
            em.op("vector", lambda e, ck=ck: e.tensor_mul(
                dst[:, ck * 512:(ck + 1) * 512], dst[:, ck * 512:(ck + 1) * 512],
                sins[:]), wait=[(AC, sa)], inc=VE)
        return em.ve_fence(c)

    for half in range(2):
        ar.mark()
        xnh = ar.tile(128, 16384, BF16)
        Vt = ar.tile(128, 8192, BF16)
        xbt = [ar.tile(128, D, BF16) for _ in range(2)]
        ldb = [None, None]
        sx = (c.LDX, c.LDX2)
        ldb[0] = em.op("sync", lambda e: e.dma_start(
            out=xbt[0][:], in_=c.s_xn1[0:128, :]),
            wait=[(VE, em.val(VE))], inc=sx[0], inc_by=16)
        for ct in range(32):
            bi = ct % 2
            if ct + 1 < 32:
                ldb[(ct + 1) % 2] = em.op("sync", lambda e, ct=ct, b=xbt[(ct + 1) % 2]:
                                          e.dma_start(out=b[:], in_=c.s_xn1[
                                              (ct + 1) * 128:(ct + 2) * 128, :]),
                                          wait=[(VE, em.val(VE))],
                                          inc=sx[(ct + 1) % 2], inc_by=16)
            em.op("vector", lambda e, ct=ct, b=xbt[bi], h=half: e.tensor_copy(
                xnh[:].rearrange("p (d t) -> p d t", t=32)[:, :, ct],
                b[:, h * 512:(h + 1) * 512]),
                wait=[(sx[bi], ldb[bi])], inc=VE)
        xnh_v = em.ve_fence(c)
        for g4 in range(4):
            G = half * 4 + g4
            vtok = build_table(Vt, G, (c.k_affV, c.k_affV), (c.s_ah, c.s_al),
                               (c.s_bh, c.s_bm, c.s_bl, c.c_shV), 2, 4, 16)
            if G == 0:
                dbg("vt0", Vt, 128, 512)
            em.wait("tensor", VE, vtok)
            em.wait("tensor", VE, xnh_v)
            for bk in range(8):
                bank = bk % 5
                em.wait("tensor", VE, em.val(VE))
                for sl in range(8):
                    pr = bk * 8 + sl
                    em.op("tensor", lambda e, pr=pr, sl=sl, bank=bank, g4=g4:
                          e.matmul(pb[bank][0:128, sl * 64:(sl + 1) * 64],
                                   Vt[:, pr * 128:(pr + 1) * 128],
                                   xnh[:, g4 * 4096 + pr * 64: g4 * 4096 + (pr + 1) * 64],
                                   start=True, stop=True))
                mmb = em.drain_inc("tensor", PE)
                p0 = G * 64 + bk * 8
                em.op("vector", lambda e, bank=bank, p0=p0: e.tensor_copy(
                    H[0:64, :].rearrange("p (pr t) -> p pr t", t=HCOL)
                    [:, p0:p0 + 8, 1:33],
                    pb[bank][0:64, 0:512].rearrange("p (s t) -> p s t", t=64)
                    [:, :, 0:32]),
                    wait=[(PE, mmb)], inc=VE)
                em.op("vector", lambda e, bank=bank, p0=p0: e.tensor_copy(
                    H[64:128, :].rearrange("p (pr t) -> p pr t", t=HCOL)
                    [:, p0:p0 + 8, 1:33],
                    pb[bank][64:128, 0:512].rearrange("p (s t) -> p s t", t=64)
                    [:, :, 32:64]),
                    inc=VE)
        ar.release()
    if getattr(c, "debug", False):
        dbg("hpre", H[:, 0:1020], 128, 1020)
    if getattr(c, "trunc", 0) == 21:
        ar.release()
        return

    ar.mark()
    Sf = ar.tile(128, 512, F32)
    Ssw = ar.tile(128, 512, F32)
    aRe = ar.tile(128, 512, F32)
    aIm = ar.tile(128, 512, F32)
    t1s = ar.tile(128, 512, F32)
    t2s = ar.tile(128, 512, F32)
    Tq = {}
    em.wait("tensor", VE, em.val(VE))
    for nm, srct in (("lqre", lq_re), ("lqim", lq_im), ("lqimn", lq_imn),
                     ("lvre", lv_re), ("lvim", lv_im)):
        Tq[nm] = ar.tile(128, 256, F32 if nm.startswith("lq") else BF16)
        if nm.startswith("lq"):
            for hh in range(2):
                em.op("tensor", lambda e, s=srct, hh=hh: e.transpose(
                    pb[5][:, hh * 128:(hh + 1) * 128],
                    s[:, hh * 128:(hh + 1) * 128], c.k_identf[:]))
            mm = em.drain_inc("tensor", PE)
            em.op("vector", lambda e, d=Tq[nm]: e.tensor_copy(d[:], pb[5][:, 0:256]),
                  wait=[(PE, mm)], inc=VE)
            em.wait("tensor", VE, em.val(VE))
        else:
            for hh in range(2):
                em.op("tensor", lambda e, s=srct, hh=hh: e.transpose(
                    pt[0:128, hh * 128:(hh + 1) * 128],
                    s[:, hh * 128:(hh + 1) * 128], c.k_ident[:]))
            mm = em.drain_inc("tensor", PE)
            em.op("vector", lambda e, d=Tq[nm]: e.tensor_copy(d[:], pt[0:128, 0:256]),
                  wait=[(PE, mm)], inc=VE)
            em.wait("tensor", VE, em.val(VE))
    for (dst, rb, nm, parity) in (
            (aRe, 0, "lqre", 0), (aRe, 32, "lqre", 0),
            (aRe, 64, "lqre", 1), (aRe, 96, "lqre", 1),
            (aIm, 0, "lqimn", 0), (aIm, 32, "lqim", 0),
            (aIm, 64, "lqimn", 1), (aIm, 96, "lqim", 1)):
        for b in range(8):
            em.op("vector", lambda e, d=dst, rb=rb, t=Tq[nm], b=b, pa=parity:
                  e.tensor_copy(
                d[rb:rb + 32, b * 64:(b + 1) * 64],
                t[(b % 4) * 32:(b % 4) * 32 + 32,
                  (b // 4) * 128 + pa:(b // 4) * 128 + pa + 127 + (1 - pa):2]),
                  inc=VE)
    for (rb, nm, parity) in ((0, "lvre", 0), (32, "lvim", 0),
                             (64, "lvre", 1), (96, "lvim", 1)):
        for b in range(8):
            em.op("vector", lambda e, rb=rb, t=Tq[nm], b=b, pa=parity:
                  e.tensor_copy(
                H[rb:rb + 32, :].rearrange("n (pr t) -> n pr t", t=HCOL)
                [:, b * 64:(b + 1) * 64, 33],
                t[(b % 4) * 32:(b % 4) * 32 + 32,
                  (b // 4) * 128 + pa:(b // 4) * 128 + pa + 127 + (1 - pa):2]),
                  inc=VE)
    em.op("vector", lambda e: e.memset(Sf[:], 0.0), inc=VE)
    em.wait("gpsimd", VE, em.val(VE))
    gp_done = None
    for st_i in range(31):
        ucol = st_i + 1
        for (a, b) in ((0, 32), (32, 0), (64, 96), (96, 64)):
            em.op("gpsimd", lambda e, a=a, b=b: e.tensor_copy(
                Ssw[a:a + 32, :], Sf[b:b + 32, :]), inc=GP)
        gp_done = em.drain_inc("gpsimd", GP)
        em.op("vector", lambda e: e.tensor_mul(t1s[:], Sf[:], aRe[:]),
              wait=[(GP, gp_done)], inc=VE)
        em.op("vector", lambda e: e.tensor_mul(t2s[:], Ssw[:], aIm[:]), inc=VE)
        em.drain("vector")
        em.op("vector", lambda e: e.tensor_tensor(t1s[:], t1s[:], t2s[:], ALU.add),
              inc=VE)
        em.drain("vector")
        em.op("vector", lambda e, u=ucol: e.tensor_tensor(
            Sf[:], t1s[:],
            H[:].rearrange("p (pr t) -> p pr t", t=HCOL)[:, :, u], ALU.add),
            inc=VE)
        em.drain("vector")
        em.op("vector", lambda e, u=ucol: e.tensor_copy(
            H[:].rearrange("p (pr t) -> p pr t", t=HCOL)[:, :, u], Sf[:]), inc=VE)
        sv = em.ve_fence(c)
        em.wait("gpsimd", VE, sv)
    ar.release()
    if getattr(c, "debug", False):
        dbg("hpost", H[:, 0:1020], 128, 1020)
    if getattr(c, "trunc", 0) == 22:
        ar.release()
        return

    ar.mark()
    H2 = ar.tile(128, NPAIR * HCOL, BF16)
    CPJ = ar.tile(128, 4096, BF16)
    CPT = ar.tile(64, 8192, BF16)
    ysg = ar.tile(128, 128 * HCOL, BF16)
    Xre = ar.tile(128, 512, BF16)
    Xim = ar.tile(128, 512, BF16)
    Yre = ar.tile(128, 512, BF16)
    Yim = ar.tile(128, 512, BF16)
    KrepR = ar.tile(128, 512, BF16)
    KrepI = ar.tile(128, 512, BF16)
    ktmp = ar.tile(128, 128, BF16)
    xcb = [ar.tile(128, 512, BF16) for _ in range(2)]
    ygel = ar.tile(128, 512, F32)
    gout = ar.tile(128, 512, BF16)
    hv = em.ve_fence(c)
    em.op("vector", lambda e: e.tensor_copy(H2[0:64, :], H[64:128, :]),
          wait=[(VE, hv)], inc=VE)
    h2v = em.ve_fence(c)

    for G in range(8):
        for hg in range(2):
            ghg = G * 2 + hg
            def hg_rows(names):
                return [getattr(c, "s_" + nm) for nm in names]
            for ck8 in range(8):
                ck = hg * 8 + ck8
                mm = affine_chunk(c.k_affM, 4,
                                  [(s, "p", G) for s in hg_rows(("ah", "al", "mh", "ml"))],
                                  ck, 5 + ck8 % 2, extra_wait=[(AC, em.val(AC))])
                em.op("scalar", lambda e, ck8=ck8, bank=5 + ck8 % 2: e.activation(
                    CPJ[:, ck8 * 512:(ck8 + 1) * 512], pb[bank][:, 0:512], AF.Exp),
                    wait=[(PE, mm)], inc=AC)
            acm = em.drain_inc("scalar", AC)
            for ck8 in range(8):
                ck = hg * 8 + ck8
                rows = ([(s, "p", G) for s in hg_rows(("bh", "bm", "bl", "gh", "gl"))]
                        + [(c.c_shC, "c", G)])
                mm = affine_chunk(c.k_affT, 6, rows, ck, 5 + ck8 % 2,
                                  extra_wait=[(VE, em.val(VE))])
                em.op("vector", lambda e, bank=5 + ck8 % 2: e.tensor_scalar_add(
                    stg5[:], pb[bank][:, 0:512], 12582912.0),
                    wait=[(PE, mm)], inc=VE)
                em.drain("vector")
                em.op("vector", lambda e: e.tensor_scalar_add(
                    stg5[:], stg5[:], -12582912.0), inc=VE)
                em.drain("vector")
                em.op("vector", lambda e, bank=5 + ck8 % 2: e.tensor_tensor(
                    stg5[:], pb[bank][:, 0:512], stg5[:], ALU.subtract), inc=VE)
                sv = em.drain_inc("vector", VE)
                em.op("scalar", lambda e: e.activation(
                    sins[:], stg5[:], AF.Sin, scale=TWOPI),
                    wait=[(VE, sv), (AC, acm)], inc=AC)
                sa = em.drain_inc("scalar", AC)
                em.op("vector", lambda e, ck8=ck8: e.tensor_mul(
                    CPJ[:, ck8 * 512:(ck8 + 1) * 512],
                    CPJ[:, ck8 * 512:(ck8 + 1) * 512], sins[:]),
                    wait=[(AC, sa)], inc=VE)
            cpv = em.ve_fence(c)
            if G == 0 and hg == 0:
                dbg("cpj0", CPJ, 128, 512)
            em.wait("tensor", VE, cpv)
            for rnd in range(8):
                em.wait("tensor", VE, em.val(VE))
                for i in range(8):
                    chn = rnd * 8 + i
                    em.op("tensor", lambda e, chn=chn, i=i: e.transpose(
                        pt[0:64, i * 128:(i + 1) * 128],
                        CPJ[:, chn * 64:(chn + 1) * 64], c.k_ident[:]))
                mmt = em.drain_inc("tensor", PE)
                em.op("vector", lambda e, rnd=rnd: e.tensor_copy(
                    CPT[0:64, rnd * 1024:(rnd + 1) * 1024], pt[0:64, 0:1024]),
                    wait=[(PE, mmt)], inc=VE)
            ctv = em.ve_fence(c)
            em.wait("tensor", VE, ctv)
            if G == 0 and hg == 0:
                em.wait("tensor", VE, h2v)
            nbank = (64 + 14) // 15
            for bank in range(nbank):
                em.wait("tensor", VE, em.val(VE))
                nch = min(15, 64 - bank * 15)
                for i in range(nch):
                    chl = bank * 15 + i
                    ch = hg * 64 + chl
                    d = G * 128 + ch
                    pair = d // 2
                    rhs_t = H if d % 2 == 0 else H2
                    em.op("tensor", lambda e, chl=chl, i=i, bank=bank,
                          rhs_t=rhs_t, pair=pair: e.matmul(
                        pb[bank][0:128, i * HCOL:(i + 1) * HCOL],
                        CPT[0:64, chl * 128:(chl + 1) * 128],
                        rhs_t[0:64, pair * HCOL:(pair + 1) * HCOL],
                        start=True, stop=True))
                mmc = em.drain_inc("tensor", PE)
                em.op("vector", lambda e, bank=bank, nch=nch, hg=hg: e.tensor_copy(
                    ysg[:, (hg * 64 + bank * 15) * HCOL:
                        (hg * 64 + bank * 15 + nch) * HCOL],
                    pb[bank][0:128, 0:nch * HCOL]),
                    wait=[(PE, mmc)], inc=VE)
        ysv = em.ve_fence(c)
        if G == 0:
            dbg("ysg0", ysg, 128, 1020)
        em.op("vector", lambda e: e.tensor_copy(
            ktmp[:], ysg[:].rearrange("p (ch t) -> p ch t", t=HCOL)[:, :, 33]),
            inc=VE)
        kv = em.ve_fence(c)
        em.wait("tensor", VE, kv)
        em.op("tensor", lambda e: e.matmul(pb[5][:, 0:128], c.k_dft_wre[:],
                                           ktmp[:], start=True, stop=True))
        em.op("tensor", lambda e: e.matmul(pb[6][:, 0:128], c.k_dft_wim[:],
                                           ktmp[:], start=True, stop=True))
        mmk = em.drain_inc("tensor", PE)
        for q in range(4):
            em.op("vector", lambda e, q=q: e.tensor_copy(
                KrepR[:, q * 128:(q + 1) * 128], pb[5][:, 0:128]),
                wait=[(PE, mmk)] if q == 0 else (), inc=VE)
            em.op("vector", lambda e, q=q: e.tensor_copy(
                KrepI[:, q * 128:(q + 1) * 128], pb[6][:, 0:128]), inc=VE)
        krv = em.ve_fence(c)
        if G == 0:
            dbg("khat0", KrepR, 128, 128)
        ldxc = [None, None]
        ldxc[0] = em.op("sync", lambda e, G=G: e.dma_start(
            out=xcb[0][:].rearrange("p (q m) -> p q m", m=128),
            in_=c.s_xn1[0:512, G * 128:(G + 1) * 128]
            .rearrange("(q p) m -> p q m", p=128)),
            wait=[(VE, em.val(VE))], inc=c.LDD, inc_by=16)
        for cb in range(8):
            bi = cb % 2
            if cb + 1 < 8:
                ldxc[(cb + 1) % 2] = em.op("sync", lambda e, cb=cb, G=G,
                                           b=xcb[(cb + 1) % 2]: e.dma_start(
                    out=b[:].rearrange("p (q m) -> p q m", m=128),
                    in_=c.s_xn1[(cb + 1) * 512:(cb + 2) * 512, G * 128:(G + 1) * 128]
                    .rearrange("(q p) m -> p q m", p=128)),
                    wait=[(VE, em.val(VE))], inc=c.LDD, inc_by=16)
            em.wait("tensor", c.LDD, ldxc[bi])
            em.wait("tensor", VE, em.val(VE))
            for q in range(4):
                em.op("tensor", lambda e, q=q, b=xcb[bi]: e.matmul(
                    pb[3][:, q * 128:(q + 1) * 128], c.k_dft_wre[:],
                    b[:, q * 128:(q + 1) * 128], start=True, stop=True))
                em.op("tensor", lambda e, q=q, b=xcb[bi]: e.matmul(
                    pb[4][:, q * 128:(q + 1) * 128], c.k_dft_wim[:],
                    b[:, q * 128:(q + 1) * 128], start=True, stop=True))
            mmf = em.drain_inc("tensor", PE)
            em.op("vector", lambda e: e.tensor_copy(Xre[:], pb[3][:, 0:512]),
                  wait=([(PE, mmf), (VE, krv)] if cb == 0 else [(PE, mmf)]), inc=VE)
            em.op("vector", lambda e: e.tensor_copy(Xim[:], pb[4][:, 0:512]), inc=VE)
            xev = em.ve_fence(c)
            em.wait("gpsimd", VE, xev)
            em.op("gpsimd", lambda e: e.tensor_mul(Yre[:], Xre[:], KrepR[:]), inc=GP)
            em.op("gpsimd", lambda e: e.tensor_mul(ygel[:], Xim[:], KrepI[:]), inc=GP)
            em.drain("gpsimd")
            em.op("gpsimd", lambda e: e.tensor_tensor(Yre[:], Yre[:], ygel[:],
                                                      ALU.subtract), inc=GP)
            gpf = em.drain_inc("gpsimd", GP)
            em.op("vector", lambda e: e.tensor_mul(Yim[:], Xre[:], KrepI[:]), inc=VE)
            em.op("vector", lambda e: e.tensor_mul(stg5[:], Xim[:], KrepR[:]), inc=VE)
            em.drain("vector")
            em.op("vector", lambda e: e.tensor_tensor(Yim[:], Yim[:], stg5[:],
                                                      ALU.add), inc=VE)
            yfv = em.ve_fence(c)
            em.wait("tensor", VE, yfv)
            em.wait("tensor", GP, gpf)
            for q in range(4):
                em.op("tensor", lambda e, q=q: e.matmul(
                    pb[2][:, q * 128:(q + 1) * 128], c.k_idft_vre[:],
                    Yre[:, q * 128:(q + 1) * 128], start=True, stop=False))
                em.op("tensor", lambda e, q=q: e.matmul(
                    pb[2][:, q * 128:(q + 1) * 128], c.k_idft_vim[:],
                    Yim[:, q * 128:(q + 1) * 128], start=False, stop=True))
            mmi = em.drain_inc("tensor", PE)
            for q in range(4):
                cidx = cb * 4 + q
                em.op("vector", lambda e, q=q, cidx=cidx: e.tensor_tensor(
                    ygel[:, q * 128:(q + 1) * 128], pb[2][:, q * 128:(q + 1) * 128],
                    ysg[:].rearrange("p (ch t) -> p t ch", t=HCOL)[:, cidx, :],
                    ALU.add),
                    wait=[(PE, mmi)] if q == 0 else (), inc=VE)
            gv = em.drain_inc("vector", VE)
            em.op("scalar", lambda e: e.activation(gout[:], ygel[:], AF.Gelu),
                  wait=[(VE, gv)], inc=AC)
            ga = em.drain_inc("scalar", AC)
            em.op("scalar", lambda e, cb=cb, G=G, b=gout: e.dma_start(
                out=c.s_gy[cb * 512:(cb + 1) * 512, G * 128:(G + 1) * 128]
                .rearrange("(q p) m -> p q m", p=128),
                in_=b[:].rearrange("p (q m) -> p q m", m=128)),
                wait=[(AC, ga)], inc=SC, inc_by=16)
    em.wait("sync", SC, em.val(SC))
    ar.release()
    ar.release()



def proj_phase(c):
    em = c.em
    LD, PE, VE, AC, GP, ST, SC = c.LD, c.PE, c.VE, c.AC, c.GP, c.ST, c.SC
    pb, pt = c.pb, c.pt
    ar = c.ar
    ar.mark()
    dbg = phase_dbg(c, ar)

    wt_sb = ar.tile(128, 8192, BF16)
    gyt = [ar.tile(128, D, BF16) for _ in range(2)]
    gyT = ar.tile(128, D, BF16)
    x1t = [ar.tile(128, D, F32) for _ in range(2)]
    woutb = ar.tile(128, D, BF16)
    x2t = [ar.tile(128, D, F32) for _ in range(2)]
    xnt = [ar.tile(128, D, BF16) for _ in range(2)]
    lns = ar.tile(128, 12, F32)
    lna = ar.tile(128, 2, F32)
    lnr1 = ar.tile(128, 1, F32)
    lns2 = ar.tile(128, 12, F32)
    lna2 = ar.tile(128, 2, F32)
    lnr2 = ar.tile(128, 1, F32)

    ldw = em.op("sync", lambda e: e.dma_start(
        out=wt_sb[:].rearrange("p (k m) -> p k m", k=8),
        in_=c.s_WT_bf[:, :].rearrange("(k p) m -> p k m", p=128)),
        wait=[(c.SC, c.prelude_sc)], inc=c.LDW, inc_by=16)
    ld_g = [None, None]
    ld_x = [None, None]
    st_2 = [None, None]
    st_n = [None, None]
    sx = (c.LDX, c.LDX2)
    sg = (c.LDG, c.LDG2)
    ld_g[0] = em.op("sync", lambda e: e.dma_start(
        out=gyt[0][:], in_=c.s_gy[0:128, :]),
        wait=[(VE, em.val(VE))], inc=sx[0], inc_by=16)
    ld_x[0] = em.op("gpsimd", lambda e: e.dma_start(
        out=x1t[0][:], in_=c.x_seq[0:128, :]),
        wait=[(VE, em.val(VE))], inc=sg[0], inc_by=16)
    for ct in range(32):
        bi = ct % 2
        if ct + 1 < 32:
            ld_g[(ct + 1) % 2] = em.op("sync", lambda e, ct=ct, b=gyt[(ct + 1) % 2]:
                                       e.dma_start(out=b[:], in_=c.s_gy[
                                           (ct + 1) * 128:(ct + 2) * 128, :]),
                                       wait=[(VE, em.val(VE))],
                                       inc=sx[(ct + 1) % 2], inc_by=16)
            ld_x[(ct + 1) % 2] = em.op("gpsimd", lambda e, ct=ct, b=x1t[(ct + 1) % 2]:
                                       e.dma_start(out=b[:], in_=c.x_seq[
                                           (ct + 1) * 128:(ct + 2) * 128, :]),
                                       wait=[(VE, em.val(VE))],
                                       inc=sg[(ct + 1) % 2], inc_by=16)
        if ct == 0:
            em.op("vector", lambda e: e.tensor_copy(c.k_fence[:], c.k_eps[:]),
                  wait=[(sx[0], ld_g[0])], inc=VE)
            em.ve_fence(c)
        em.wait("tensor", sx[bi], ld_g[bi])
        em.wait("tensor", VE, em.val(VE))
        for k in range(8):
            em.op("tensor", lambda e, k=k, b=gyt[bi]: e.transpose(
                pt[0:128, k * 128:(k + 1) * 128], b[:, k * 128:(k + 1) * 128],
                c.k_ident[:]))
        mmt = em.drain_inc("tensor", PE)
        em.op("vector", lambda e: e.tensor_copy(gyT[:], pt[0:128, 0:1024]),
              wait=[(PE, mmt)], inc=VE)
        gv = em.ve_fence(c)
        em.wait("tensor", VE, gv)
        if ct == 0:
            em.wait("tensor", c.LDW, ldw)
        for hf in range(2):
            for k in range(8):
                em.op("tensor", lambda e, hf=hf, k=k: e.matmul(
                    pb[hf][:, 0:512], gyT[:, k * 128:(k + 1) * 128],
                    wt_sb[:, k * 1024 + hf * 512: k * 1024 + hf * 512 + 512],
                    start=(k == 0), stop=False))
            em.op("tensor", lambda e, hf=hf: e.matmul(
                pb[hf][:, 0:512], c.k_ones128[:],
                c.k_bias128[:, hf * 512:(hf + 1) * 512], start=False, stop=True))
        mmw = em.drain_inc("tensor", PE)
        if ct == 0:
            dbg("wout0", pb[0][0:128, 0:512], 128, 512, wait=[(PE, mmw)])
            dbg("gyT0", gyT, 128, 512)
        em.op("vector", lambda e: e.bn_stats(lns[:, 0:6], pb[0][:, 0:512]),
              wait=[(PE, mmw)], inc=VE)
        em.op("vector", lambda e: e.bn_stats(lns[:, 6:12], pb[1][:, 0:512]), inc=VE)
        em.drain("vector")
        em.op("vector", lambda e: e.bn_aggr(lna[:], lns[:]), inc=VE)
        bn = em.drain_inc("vector", VE)
        em.op("scalar", lambda e: e.activation(lnr1[:], lna[:, 1:2], AF.Ln,
                                               bias=c.k_eps[:]),
              wait=[(VE, bn)], inc=AC)
        em.drain("scalar")
        em.op("scalar", lambda e: e.activation(lnr1[:], lnr1[:], AF.Exp,
                                               scale=-0.5), inc=AC)
        af = em.drain_inc("scalar", AC)
        ws = [(AC, af), (sg[bi], ld_x[bi])]
        if st_2[bi] is not None:
            ws.append(((c.STP, c.STP2)[bi], st_2[bi]))
        for hf in range(2):
            em.op("vector", lambda e, hf=hf, o=x2t[bi]: e.tensor_scalar(
                o[:, hf * 512:(hf + 1) * 512], pb[hf][:, 0:512],
                lna[:, 0:1], lnr1[:], ALU.subtract, ALU.mult),
                wait=ws if hf == 0 else (), inc=VE)
        em.drain("vector")
        em.op("vector", lambda e, o=x2t[bi], b=x1t[bi]: e.tensor_tensor(
            o[:], o[:], b[:], ALU.add), inc=VE)
        fv = em.ve_fence(c)
        st_2[bi] = em.op("sync", lambda e, ct=ct, o=x2t[bi]: e.dma_start(
            out=c.s_x2[ct * 128:(ct + 1) * 128, :], in_=o[:]),
            wait=[(VE, fv)], inc=(c.STP, c.STP2)[bi], inc_by=16)
        em.op("vector", lambda e, o=x2t[bi]: e.bn_stats(lns2[:, 0:6], o[:, 0:512]),
              inc=VE)
        em.op("vector", lambda e, o=x2t[bi]: e.bn_stats(lns2[:, 6:12], o[:, 512:1024]),
              inc=VE)
        em.drain("vector")
        em.op("vector", lambda e: e.bn_aggr(lna2[:], lns2[:]), inc=VE)
        bn2 = em.drain_inc("vector", VE)
        em.op("scalar", lambda e: e.activation(lnr2[:], lna2[:, 1:2], AF.Ln,
                                               bias=c.k_eps[:]),
              wait=[(VE, bn2)], inc=AC)
        em.drain("scalar")
        em.op("scalar", lambda e: e.activation(lnr2[:], lnr2[:], AF.Exp,
                                               scale=-0.5), inc=AC)
        af2 = em.drain_inc("scalar", AC)
        ws2 = [(AC, af2)]
        if st_n[bi] is not None:
            ws2.append(((c.STN, c.STN2)[bi], st_n[bi]))
        em.op("vector", lambda e, o=x2t[bi], n=xnt[bi]: e.tensor_scalar(
            n[:], o[:], lna2[:, 0:1], lnr2[:], ALU.subtract, ALU.mult),
            wait=ws2, inc=VE)
        fv2 = em.ve_fence(c)
        st_n[bi] = em.op("sync", lambda e, ct=ct, n=xnt[bi]: e.dma_start(
            out=c.s_xn2[ct * 128:(ct + 1) * 128, :], in_=n[:]),
            wait=[(VE, fv2)], inc=(c.STN, c.STN2)[bi], inc_by=16)
    for s in (c.ST, c.STP, c.STP2, c.STN, c.STN2):
        em.wait("sync", s, em.val(s))
    ar.release()


def ffn_test_prep(c):
    em, LD, VE, SC = c.em, c.LD, c.VE, c.SC
    c.t_xn2 = c.din("t_xn2", [L, D])
    c.t_x2 = c.din("t_x2", [L, D])
    c.ar.mark()
    stg = c.ar.tile(128, 4096, F32)
    stgb = c.ar.tile(128, 4096, BF16)
    for k in range(L // 128):
        ld = em.op("sync", lambda e, k=k: e.dma_start(
            out=stg[:, 0:1024], in_=c.t_xn2[k * 128:(k + 1) * 128, :]),
            wait=[(SC, em.val(SC))], inc=LD, inc_by=16)
        em.op("vector", lambda e: e.tensor_copy(stgb[:, 0:1024], stg[:, 0:1024]),
             wait=[(LD, ld)], inc=VE)
        cv = em.ve_fence(c)
        em.op("scalar", lambda e, k=k: e.dma_start(
            out=c.s_xn2[k * 128:(k + 1) * 128, :], in_=stgb[:, 0:1024]),
            wait=[(VE, cv)], inc=SC, inc_by=16)
    for k in range(L // 128):
        ld = em.op("sync", lambda e, k=k: e.dma_start(
            out=stg[:, 0:1024], in_=c.t_x2[k * 128:(k + 1) * 128, :]),
            wait=[(SC, em.val(SC))], inc=LD, inc_by=16)
        em.op("scalar", lambda e, k=k: e.dma_start(
            out=c.s_x2[k * 128:(k + 1) * 128, :], in_=stg[:, 0:1024]),
            wait=[(LD, ld)], inc=SC, inc_by=16)
    c.ar.release()
    em.wait("sync", SC, em.val(SC))


def finish(c):
    em = c.em
    for s in (c.ST, c.STP, c.STP2, c.STN, c.STN2):
        em.wait("sync", s, em.val(s))
        em.wait("gpsimd", s, em.val(s))


def build(mode="full", debug=False, trunc=0):
    c = make_ctx(mode)
    c.debug = debug
    c.trunc = trunc
    import os as _os
    c.no_prefetch = (trunc in (1, 2, 3, 4)) or bool(_os.environ.get("NOPREF"))
    c.slow_tr = (trunc == 3)
    c.skip_bigcast = (trunc == 4)
    with c.es:
        with c.nc.Block() as block:
            if debug:
                c._dbg_tiles = [c.ar.tile(128, 1024, F32) for _ in range(2)]
                c._dbg_i = 0
                c._dbg_st = [None, None]
            prelude(c)
            if mode == "ffn":
                ffn_test_prep(c)
                ffn_phase(c)
            elif mode == "ffn_ext":
                ffn_phase(c)
            elif mode == "ssm":
                ssm_phase(c)
            elif mode == "p2":
                ssm_phase(c)
                proj_phase(c)
            elif mode == "full":
                ssm_phase(c)
                proj_phase(c)
                ffn_phase(c)
            else:
                raise NotImplementedError(mode)
            finish(c)
            c.em.run_all(block)
    return c


def kernel(**inputs):
    maps = host_prepare(inputs)
    c = build("full")
    res = run_bass_kernel_spmd(c.nc, maps, core_ids=list(range(8)))
    out = np.stack([res.results[i]["out"] for i in range(8)], axis=0)
    return out


def build_trunc():
    return build("ffn", debug=True, trunc=1)
